# version 16
# baseline (speedup 1.0000x reference)
import contextlib
import numpy as np
import ml_dtypes
import concourse.bass as bass
import concourse.mybir as mybir
from concourse.bass_utils import run_bass_kernel_spmd

F32 = mybir.dt.float32
BF16 = mybir.dt.bfloat16
I32 = mybir.dt.int32
AF = mybir.ActivationFunctionType
OP = mybir.AluOpType
AX = mybir.AxisListType

B, S, D = 8, 2048, 2048
NH, HD, KVH = 16, 128, 4
WIN = 512
II = 8192
NT = S // 128
ND = D // 128
NI = II // 128
SCALE_ATT = float(1.0 / np.sqrt(128.0))
LN_EPS = 1e-5
RMS_EPS = 1.1920928955078125e-07

PHASES = ("A", "A2", "B0", "B0S", "B", "C", "C2", "D")

ENGS = ("sync", "tensor", "vector", "scalar", "gpsimd")


class Ph:
    def __init__(self, nc):
        self.nc = nc
        self.q = {e: [] for e in ENGS}

    def em(self, eng, fn):
        self.q[eng].append(fn)

    def run(self):
        with self.nc.Block() as block:
            for e in ENGS:
                ops = self.q[e]
                if ops:
                    getattr(block, e)(lambda eng, ops=ops: [f(eng) for f in ops])


class Sem:
    def __init__(self, stack, nc, name):
        self.h = stack.enter_context(nc.semaphore(name))
        self.n = 0


def E(ph, eng, build, waits=(), inc=None, inck=1):
    if inc is not None:
        inc.n += inck
    def f(e, build=build, waits=tuple(waits), inc=inc, inck=inck):
        for s_, v_ in waits:
            e.wait_ge(s_.h, v_)
        inst = build(e)
        if inc is not None:
            inst.then_inc(inc.h, inck)
    ph.em(eng, f)
    return inc.n if inc is not None else None


def W(ph, eng, waits):
    def f(e, waits=tuple(waits)):
        for s_, v_ in waits:
            e.wait_ge(s_.h, v_)
    ph.em(eng, f)


def _build(phases=PHASES):
    nc = bass.Bass("TRN2", target_bir_lowering=False)
    st = contextlib.ExitStack()

    xall = nc.dram_tensor("xall", [B * S, D], F32, kind="ExternalInput").ap()
    xc = nc.dram_tensor("xc", [S, D], F32, kind="ExternalInput").ap()
    wqkvT = nc.dram_tensor("wqkvT", [D, 768], F32, kind="ExternalInput").ap()
    wprojT = nc.dram_tensor("wprojT", [D, D], F32, kind="ExternalInput").ap()
    bprojT = nc.dram_tensor("bprojT", [1, D], F32, kind="ExternalInput").ap()
    wqT = nc.dram_tensor("wqT", [D, D], F32, kind="ExternalInput").ap()
    wkT = nc.dram_tensor("wkT", [D, 512], F32, kind="ExternalInput").ap()
    wvT = nc.dram_tensor("wvT", [D, 512], F32, kind="ExternalInput").ap()
    woT = nc.dram_tensor("woT", [D, D], F32, kind="ExternalInput").ap()
    U8 = mybir.dt.uint8
    gwT = nc.dram_tensor("gwT", [D, II], U8, kind="ExternalInput").ap()
    uwT = nc.dram_tensor("uwT", [D, II], U8, kind="ExternalInput").ap()
    dwT = nc.dram_tensor("dwT", [II, D], U8, kind="ExternalInput").ap()
    gs_col = nc.dram_tensor("gs_col", [128, NI], F32, kind="ExternalInput").ap()
    us_col = nc.dram_tensor("us_col", [128, NI], F32, kind="ExternalInput").ap()
    ds_bc = nc.dram_tensor("ds_bc", [128, D], F32, kind="ExternalInput").ap()
    qg_bc = nc.dram_tensor("qg_bc", [128, NH], F32, kind="ExternalInput").ap()
    identb = nc.dram_tensor("identb", [128, 128], BF16, kind="ExternalInput").ap()
    ropec = nc.dram_tensor("ropec", [S, 64], F32, kind="ExternalInput").ap()
    ropes = nc.dram_tensor("ropes", [S, 64], F32, kind="ExternalInput").ap()
    maskb = nc.dram_tensor("maskb", [8 * 128, 512], BF16, kind="ExternalInput").ap()
    ones_row = nc.dram_tensor("ones_row", [1, 128], BF16, kind="ExternalInput").ap()
    ones_col = nc.dram_tensor("ones_col", [128, 1], BF16, kind="ExternalInput").ap()

    y = nc.dram_tensor("y", [S, D], F32, kind="ExternalOutput").ap()

    xs = nc.dram_tensor("xs", [S, D], F32, kind="Internal").ap()
    wgscr = nc.dram_tensor("wgscr", [NI * ND * 128, 128], BF16, kind="Internal").ap()
    wuscr = nc.dram_tensor("wuscr", [NI * ND * 128, 128], BF16, kind="Internal").ap()
    wdscr = nc.dram_tensor("wdscr", [4 * NI * 128, 512], BF16, kind="Internal").ap()
    ctxb_scr = nc.dram_tensor("ctxb_scr", [NH * 128, S], BF16, kind="Internal").ap()

    sb = lambda name, shp, dt: st.enter_context(nc.sbuf_tensor(name, shp, dt))
    ident_sb = sb("ident_sb", [128, 128], BF16)
    gs_sb = sb("gs_sb", [128, NI], F32)
    us_sb = sb("us_sb", [128, NI], F32)
    qg_sb = sb("qg_sb", [128, NH], F32)
    onesr_sb = sb("onesr_sb", [1, 128], BF16)
    onesc_sb = sb("onesc_sb", [128, 1], BF16)
    bprojb_sb = sb("bprojb_sb", [1, D], BF16)
    colbuf = sb("colbuf", [128, 64], F32)
    rowbuf = sb("rowbuf", [1, 2048], F32)
    statb = sb("statb", [128, 128], F32)
    bcol = sb("bcol", [128, 18], F32)
    bb16 = sb("bb16", [1, 2], BF16)

    pf = st.enter_context(nc.psum_tensor("pf", [128, 3072], F32))
    pb = st.enter_context(nc.psum_tensor("pb", [128, 2048], BF16))
    PF = lambda i: pf[:, i * 512:(i + 1) * 512]
    PB = lambda h: pb[:, h * 1024:(h + 1) * 1024]

    sems = {}
    def SM(name):
        if name not in sems:
            sems[name] = Sem(st, nc, name)
        return sems[name]

    ph = Ph(nc)
    ldc = SM("ldc")
    def ld(dst, src, eng="sync"):
        E(ph, eng, lambda e, d=dst, s=src: e.dma_start(out=d, in_=s), inc=ldc, inck=16)
    ld(ident_sb[:], identb[:])
    ld(gs_sb[:], gs_col[:])
    ld(us_sb[:], us_col[:])
    ld(qg_sb[:], qg_bc[:])
    ld(onesr_sb[:], ones_row[:])
    ld(onesc_sb[:], ones_col[:])
    ld(bprojb_sb[:], bprojT[:], eng="gpsimd")
    W(ph, "sync", [(ldc, ldc.n)])
    W(ph, "gpsimd", [(ldc, ldc.n)])
    ph.run()

    def emit_ln_tile(ph, sm, xsrc_ap, xbuf, lnxbuf, dst3_fn, nbuf=2):
        ldx, vst, sln, apl, ttr, tev = (sm[k] for k in ("ldx", "vst", "sln", "apl", "ttr", "tev"))
        vld = E(ph, "sync", lambda e, d=xbuf, s=xsrc_ap: e.dma_start(out=d[:], in_=s),
                waits=[(apl, max(0, apl.n - (nbuf - 1)))], inc=ldx, inck=16)
        for j in range(4):
            E(ph, "vector", lambda e, j=j, xb=xbuf: e.bn_stats(statb[:, j * 6:j * 6 + 6],
                                                               xb[:, j * 512:(j + 1) * 512]),
              waits=([(ldx, vld)] if j == 0 else ()), inc=(vst if j == 3 else None))
        vagg = E(ph, "vector", lambda e: e.bn_aggr(statb[:, 24:26], statb[:, 0:24]),
                 waits=[(vst, vst.n)], inc=vst)
        E(ph, "vector", lambda e: e.tensor_scalar(out=statb[:, 26:27], in0=statb[:, 24:25],
                                                  scalar1=-1.0, scalar2=None, op0=OP.mult),
          waits=[(vst, vagg)])
        vve = E(ph, "vector", lambda e: e.tensor_scalar(out=statb[:, 27:28], in0=statb[:, 25:26],
                                                        scalar1=LN_EPS, scalar2=None, op0=OP.add),
                inc=vst)
        s1 = E(ph, "scalar", lambda e: e.activation(out=statb[:, 28:29], in_=statb[:, 27:28], func=AF.Ln),
               waits=[(vst, vve)], inc=sln)
        s2 = E(ph, "scalar", lambda e: e.activation(out=statb[:, 29:30], in_=statb[:, 28:29],
                                                    func=AF.Exp, scale=-0.5),
               waits=[(sln, s1)], inc=sln)
        vn = E(ph, "vector", lambda e: e.tensor_tensor(out=statb[:, 30:31], in0=statb[:, 26:27],
                                                       in1=statb[:, 29:30], op=OP.mult),
               waits=[(sln, s2)], inc=vst)
        va = E(ph, "vector", lambda e, xb=xbuf, lb=lnxbuf: e.tensor_scalar(
                   out=lb[:], in0=xb[:], scalar1=statb[:, 29:30], scalar2=statb[:, 30:31],
                   op0=OP.mult, op1=OP.add),
               waits=[(vst, vn), (ttr, ttr.n)], inc=apl)
        for h in range(2):
            wts = ([(apl, va)] if h == 0 else []) + [(tev, tev.n)]
            for j in range(8):
                dtile = h * 8 + j
                E(ph, "tensor", lambda e, h=h, j=j, dtile=dtile, lb=lnxbuf: e.transpose(
                      PB(h)[:, j * 128:(j + 1) * 128], lb[:, dtile * 128:(dtile + 1) * 128], ident_sb[:]),
                  waits=(wts if j == 0 else ()), inc=(ttr if j == 7 else None))
            E(ph, "vector", lambda e, h=h, dst3=dst3_fn(h): e.tensor_copy(
                  out=dst3, in_=PB(h)[:].rearrange("p (j f) -> p j f", j=8)),
              waits=[(ttr, ttr.n)], inc=tev)

    def emit_ln_chunk(ph, sm, src_aps, xbufs, lnxbufs, dst3_fns, apl_vals, ttr_vals):
        ldx, vst, sln, apl, ttr, tev = (sm[k] for k in ("ldx", "vst", "sln", "apl", "ttr", "tev"))
        dvals = []
        stat_vals = []
        for i in range(4):
            k = len(apl_vals) - 4 + i
            gw = [(apl, apl_vals[k])] if k >= 0 else []
            dvals.append(E(ph, "sync", lambda e, d=xbufs[i], s=src_aps[i]: e.dma_start(out=d[:], in_=s),
                           waits=gw, inc=ldx, inck=16))
        for i in range(4):
            for j in range(4):
                E(ph, "vector", lambda e, i=i, j=j, xb=xbufs[i]: e.bn_stats(
                      statb[:, i * 24 + j * 6:i * 24 + j * 6 + 6], xb[:, j * 512:(j + 1) * 512]),
                  waits=([(ldx, dvals[i])] if j == 0 else ()),
                  inc=(vst if j == 3 else None))
            stat_vals.append(vst.n)
        for i in range(4):
            E(ph, "vector", lambda e, i=i: e.bn_aggr(statb[:, 96 + 2 * i:98 + 2 * i],
                                                     statb[:, i * 24:i * 24 + 24]),
              waits=[(vst, stat_vals[i])], inc=(vst if i == 3 else None))
        agg = vst.n
        mv3 = statb[:, 96:104].rearrange("p (a f) -> p a f", f=2)
        E(ph, "vector", lambda e, mv3=mv3: e.tensor_scalar(
              out=statb[:, 116:120], in0=mv3[:, :, 0:1], scalar1=-1.0, scalar2=None, op0=OP.mult),
          waits=[(vst, agg)])
        vv = E(ph, "vector", lambda e, mv3=mv3: e.tensor_scalar(
                   out=statb[:, 104:108], in0=mv3[:, :, 1:2], scalar1=LN_EPS, scalar2=None, op0=OP.add),
               inc=vst)
        s1 = E(ph, "scalar", lambda e: e.activation(out=statb[:, 108:112], in_=statb[:, 104:108], func=AF.Ln),
               waits=[(vst, vv)], inc=sln)
        s2 = E(ph, "scalar", lambda e: e.activation(out=statb[:, 112:116], in_=statb[:, 108:112],
                                                    func=AF.Exp, scale=-0.5),
               waits=[(sln, s1)], inc=sln)
        vn = E(ph, "vector", lambda e: e.tensor_tensor(out=statb[:, 120:124], in0=statb[:, 116:120],
                                                       in1=statb[:, 112:116], op=OP.mult),
               waits=[(sln, s2)], inc=vst)
        for i in range(4):
            kb = len(ttr_vals) - 2
            tw = [(ttr, ttr_vals[kb])] if kb >= 0 else []
            va = E(ph, "vector", lambda e, i=i, xb=xbufs[i], lb=lnxbufs[i % 2]: e.tensor_scalar(
                       out=lb[:], in0=xb[:], scalar1=statb[:, 112 + i:113 + i],
                       scalar2=statb[:, 120 + i:121 + i], op0=OP.mult, op1=OP.add),
                   waits=[(vst, vn)] + tw, inc=apl)
            apl_vals.append(va)
            for h in range(2):
                wts = ([(apl, va)] if h == 0 else []) + [(tev, tev.n)]
                for j in range(8):
                    dtile = h * 8 + j
                    E(ph, "tensor", lambda e, h=h, j=j, dtile=dtile, lb=lnxbufs[i % 2]: e.transpose(
                          PB(h)[:, j * 128:(j + 1) * 128], lb[:, dtile * 128:(dtile + 1) * 128],
                          ident_sb[:]),
                      waits=(wts if j == 0 else ()), inc=(ttr if j == 7 else None))
                E(ph, "vector", lambda e, h=h, dst3=dst3_fns[i](h): e.tensor_copy(
                      out=dst3, in_=PB(h)[:].rearrange("p (j f) -> p j f", j=8)),
                  waits=[(ttr, ttr.n)], inc=tev)
            ttr_vals.append(ttr.n)

    if "A" in phases:
        stAk = contextlib.ExitStack()
        ctxaT = stAk.enter_context(nc.sbuf_tensor("ctxaT", [128, 16 * 2048], BF16))

        stA = contextlib.ExitStack()
        sbA = lambda name, shp, dt: stA.enter_context(nc.sbuf_tensor(name, shp, dt))
        wqkv_sb = sbA("wqkv_sb", [128, ND * 768], BF16)
        qT_sb = [sbA(f"qTa{h}", [128, S], BF16) for h in range(2)]
        kT_sb = [sbA(f"kTa{h}", [128, S], BF16) for h in range(2)]
        vaug = [sbA(f"vauga{h}", [128, NT * 132], BF16) for h in range(2)]
        lnxT = sbA("lnxTa", [128, ND * 512], BF16)
        xbuf = [sbA(f"xbufA{i}", [128, D], F32) for i in range(4)]
        lnxbuf = [sbA(f"lnxbufA{i}", [128, D], BF16) for i in range(2)]
        sqtmp = sbA("sqtmpA", [128, S], BF16)
        ptb = [sbA(f"ptbA{i}", [128, 1024], BF16) for i in range(3)]
        ctx_sb = sbA("ctx_sbA", [128, 512], BF16)
        vtmp = sbA("vtmpA", [128, 512], BF16)

        b0 = "B0" in phases and "B0S" not in phases
        if b0:
            wib = [stA.enter_context(nc.sbuf_tensor(f"wib{i}", [128, II // 4], mybir.dt.uint8))
                   for i in range(2)]
            wbb = [stA.enter_context(nc.sbuf_tensor(f"wbb{i}", [128, II // 4], BF16))
                   for i in range(2)]

        ph = Ph(nc)
        ldw = SM("ldwA")
        smA = dict(ldx=SM("ldxA"), vst=SM("vstA"), sln=SM("slnA"), apl=SM("aplA"),
                   ttr=SM("ttrA"), tev=SM("tevA"))
        ldx, vst, sln, apl, ttr, tev = (smA[k] for k in ("ldx", "vst", "sln", "apl", "ttr", "tev"))
        tqk = SM("tqkA"); eqk = SM("eqkA")
        tS = SM("tSA"); sP = SM("sPA"); tC = SM("tCA")
        vC = SM("vCA"); tT = SM("tTA"); vE = SM("vEA"); nrm = SM("nrmA")

        E(ph, "gpsimd", lambda e: e.dma_start(
              out=wqkv_sb[:].rearrange("p (a f) -> p a f", a=ND),
              in_=wqkvT.rearrange("(a p) f -> p a f", p=128)),
          inc=ldw, inck=16)
        W(ph, "tensor", [(ldw, ldw.n)])

        lnxT3 = lnxT[:].rearrange("p (d f) -> p d f", d=ND)
        va3s = [vaug[h][:].rearrange("p (a f) -> p a f", a=NT) for h in range(2)]

        if b0:
            ldq = SM("ldqB0"); dqc = SM("dqcB0"); spl = SM("splB0")
            wg4 = wgscr.rearrange("(ii d p) f -> d p ii f", ii=NI, p=128)
            wu4 = wuscr.rearrange("(ii d p) f -> d p ii f", ii=NI, p=128)
            wd4 = wdscr.rearrange("(ec ii p) f -> ii p ec f", ec=4, p=128)
            b0_jobs = ([("g", dt, hf) for dt in range(ND) for hf in range(4)]
                       + [("u", dt, hf) for dt in range(ND) for hf in range(4)]
                       + [("d", it, hf) for it in range(NI) for hf in range(4)])
            b0_state = {"idx": 0, "spl_after": [], "deq_after": []}

            def emit_b0_job():
                idx = b0_state["idx"]
                if idx >= len(b0_jobs):
                    return
                b0_state["idx"] += 1
                mat, t, hf = b0_jobs[idx]
                bi = idx % 2
                ncols = II // 4 if mat in ("g", "u") else D // 4
                src = {"g": gwT, "u": uwT, "d": dwT}[mat]
                lwts = []
                if idx >= 2:
                    lwts.append((dqc, b0_state["deq_after"][idx - 2]))
                vld = E(ph, "sync", lambda e, bi=bi, src=src, t=t, hf=hf, ncols=ncols: e.dma_start(
                            out=wib[bi][:, 0:ncols],
                            in_=src[t * 128:(t + 1) * 128, hf * ncols:(hf + 1) * ncols]),
                        waits=lwts, inc=ldq, inck=16)
                wts = [(ldq, vld)]
                if idx >= 2:
                    wts.append((spl, b0_state["spl_after"][idx - 2]))
                vdq = E(ph, "vector", lambda e, bi=bi, ncols=ncols: e.tensor_scalar(
                            out=wbb[bi][:, 0:ncols], in0=wib[bi][:, 0:ncols],
                            scalar1=-128.0, scalar2=None, op0=OP.add),
                        waits=wts, inc=dqc)
                b0_state["deq_after"].append(dqc.n)
                if mat in ("g", "u"):
                    dst4 = wg4 if mat == "g" else wu4
                    E(ph, "sync", lambda e, bi=bi, dst4=dst4, t=t, hf=hf: e.dma_start(
                          out=dst4[t][:, hf * 16:(hf + 1) * 16, :],
                          in_=wbb[bi][:, 0:II // 4].rearrange("p (a f) -> p a f", a=NI // 4)),
                      waits=[(dqc, vdq)], inc=spl, inck=16)
                else:
                    E(ph, "sync", lambda e, bi=bi, t=t, hf=hf: e.dma_start(
                          out=wd4[t][:, hf:hf + 1, :],
                          in_=wbb[bi][:, 0:D // 4].rearrange("p (a f) -> p a f", a=1)),
                      waits=[(dqc, vdq)], inc=spl, inck=16)
                b0_state["spl_after"].append(spl.n)

        apl_valsA = []
        ttr_valsA = []
        for m in range(B):
            for u in range(4):
                emit_ln_chunk(
                    ph, smA,
                    [xall[(m * S + u * 512 + t4 * 128):(m * S + u * 512 + t4 * 128 + 128), :]
                     for t4 in range(4)],
                    xbuf, lnxbuf,
                    [(lambda h, t4=t4: lnxT3[:, h * 8:(h + 1) * 8, t4 * 128:(t4 + 1) * 128])
                     for t4 in range(4)],
                    apl_valsA, ttr_valsA)
                if b0:
                    for _ in range(12):
                        emit_b0_job()
                for o in range(6):
                    bk = o % 2
                    wts = []
                    if o == 0:
                        wts.append((tev, tev.n))
                    wts.append((eqk, max(0, eqk.n - 1)))
                    for dt in range(ND):
                        E(ph, "tensor", lambda e, bk=bk, dt=dt, o=o: e.matmul(
                              PF(bk), wqkv_sb[:, dt * 768 + o * 128: dt * 768 + o * 128 + 128],
                              lnxT[:, dt * 512:(dt + 1) * 512],
                              start=(dt == 0), stop=(dt == ND - 1)),
                          waits=(wts if dt == 0 else ()), inc=(tqk if dt == ND - 1 else None))
                    h = 0 if o < 3 else 1
                    kind = o % 3
                    if kind == 0:
                        dst = qT_sb[h][:, u * 512:(u + 1) * 512]
                    elif kind == 1:
                        dst = kT_sb[h][:, u * 512:(u + 1) * 512]
                    else:
                        dst = vtmp[:]
                    ewts = [(tqk, tqk.n)]
                    if kind == 2:
                        ewts.append((ttr, ttr.n))
                    E(ph, "scalar", lambda e, dst=dst, bk=bk: e.copy(out=dst, in_=PF(bk)),
                      waits=ewts, inc=eqk)
                    if kind == 2:
                        for j in range(4):
                            E(ph, "tensor", lambda e, j=j: e.transpose(
                                  PB(0)[:, j * 128:(j + 1) * 128], vtmp[:, j * 128:(j + 1) * 128],
                                  ident_sb[:]),
                              waits=([(eqk, eqk.n), (tev, tev.n)] if j == 0 else ()),
                              inc=(ttr if j == 3 else None))
                        E(ph, "vector", lambda e, h=h, u=u, va3=va3s[h]: e.tensor_copy(
                              out=va3[:, u * 4:(u + 1) * 4, 0:128],
                              in_=pb[:, 0:512].rearrange("p (j f) -> p j f", j=4)),
                          waits=[(ttr, ttr.n)], inc=tev)
            for h in range(2):
                E(ph, "vector", lambda e, h=h, va3=va3s[h]: e.memset(va3[:, :, 128:129], 1.0),
                  waits=[(tev, tev.n)], inc=tev)

            for h in range(2):
                g2 = 2 * m + h
                vq = E(ph, "vector", lambda e, h=h: e.tensor_tensor(
                           out=sqtmp[:], in0=qT_sb[h][:], in1=qT_sb[h][:], op=OP.mult),
                       waits=[(eqk, eqk.n), (tev, tev.n)], inc=nrm)
                for j in range(4):
                    E(ph, "tensor", lambda e, j=j: e.matmul(
                          pf[0:1, j * 512:j * 512 + 512], onesc_sb[:],
                          sqtmp[:, j * 512:(j + 1) * 512], start=True, stop=True),
                      waits=([(nrm, vq), (vC, vC.n), (sP, sP.n)] if j == 0 else ()),
                      inc=(tS if j == 3 else None))
                vqe = E(ph, "vector", lambda e: e.tensor_copy(out=rowbuf[0:1, :], in_=pf[0:1, 0:2048]),
                        waits=[(tS, tS.n)], inc=nrm)
                vqm = E(ph, "vector", lambda e: e.reduce_max(colbuf[0:1, 40:41], rowbuf[0:1, :], axis=AX.X),
                        waits=[(nrm, vqe)], inc=nrm)
                vk = E(ph, "vector", lambda e, h=h: e.tensor_tensor(
                           out=sqtmp[:], in0=kT_sb[h][:], in1=kT_sb[h][:], op=OP.mult),
                      waits=[(nrm, vqm)], inc=nrm)
                for j in range(4):
                    E(ph, "tensor", lambda e, j=j: e.matmul(
                          pf[0:1, j * 512:j * 512 + 512], onesc_sb[:],
                          sqtmp[:, j * 512:(j + 1) * 512], start=True, stop=True),
                      waits=([(nrm, vk)] if j == 0 else ()), inc=(tS if j == 3 else None))
                vke = E(ph, "vector", lambda e: e.tensor_copy(out=rowbuf[0:1, :], in_=pf[0:1, 0:2048]),
                        waits=[(tS, tS.n)], inc=nrm)
                vkm = E(ph, "vector", lambda e: e.reduce_max(colbuf[0:1, 41:42], rowbuf[0:1, :], axis=AX.X),
                        waits=[(nrm, vke)], inc=nrm)
                vp = E(ph, "vector", lambda e: e.tensor_tensor(out=colbuf[0:1, 42:43],
                                                               in0=colbuf[0:1, 40:41],
                                                               in1=colbuf[0:1, 41:42], op=OP.mult),
                      waits=[(nrm, vkm)], inc=nrm)
                sl = E(ph, "scalar", lambda e: e.activation(out=colbuf[0:1, 43:44], in_=colbuf[0:1, 42:43],
                                                            func=AF.Ln),
                       waits=[(nrm, vp)], inc=sln)
                se = E(ph, "scalar", lambda e: e.activation(out=colbuf[0:1, 44:45], in_=colbuf[0:1, 43:44],
                                                            func=AF.Exp, scale=0.5),
                       waits=[(sln, sl)], inc=sln)
                vb = E(ph, "vector", lambda e: e.tensor_scalar(out=colbuf[0:1, 45:46], in0=colbuf[0:1, 44:45],
                                                               scalar1=float(-1.01 * SCALE_ATT),
                                                               scalar2=None, op0=OP.mult),
                      waits=[(sln, se)], inc=nrm)
                vbc = E(ph, "vector", lambda e: e.tensor_copy(out=bb16[0:1, 0:1], in_=colbuf[0:1, 45:46]),
                        waits=[(nrm, vb)], inc=nrm)
                E(ph, "tensor", lambda e: e.matmul(pf[:, 0:1], onesr_sb[:], bb16[0:1, 0:1],
                                                   start=True, stop=True),
                  waits=[(nrm, vbc)], inc=tS)
                vbe = E(ph, "vector", lambda e, h=h: e.tensor_copy(out=bcol[:, h:h + 1], in_=pf[:, 0:1]),
                        waits=[(tS, tS.n)], inc=nrm)

                CTX = lambda j: pf[:, (4 + j // 2) * 512 + (j % 2) * 130:
                                   (4 + j // 2) * 512 + (j % 2) * 130 + 129]
                CTXSUM = lambda j: pf[:, (4 + j // 2) * 512 + (j % 2) * 130 + 128:
                                      (4 + j // 2) * 512 + (j % 2) * 130 + 129]
                NP = NT // 2

                def emit_ctx_pair(p, h, exp_val):
                    for half in range(2):
                        kk = 2 * p + half
                        for j in range(4):
                            E(ph, "tensor", lambda e, j=j, half=half, kk=kk, p=p, h=h: e.matmul(
                                  CTX(j), ptb[p % 3][:, half * 512 + j * 128: half * 512 + (j + 1) * 128],
                                  vaug[h][:, kk * 132:kk * 132 + 129],
                                  start=(kk == 0 and j % 2 == 0), stop=(kk == NT - 1)),
                              waits=([(sP, exp_val)] if (half == 0 and j == 0) else ()),
                              inc=(tC if (half == 1 and j == 3) else None))

                for qc in range(4):
                    q0 = qc * 512
                    exp_vals = []
                    ctx_after = []
                    tC_base = tC.n
                    for p in range(NP):
                        bk = 2 * (p % 2)
                        for half in range(2):
                            kk = 2 * p + half
                            wts = []
                            if half == 0:
                                wts.append((sP, max(0, sP.n - 1)))
                            if p == 0 and half == 0:
                                wts.append((vC, vC.n))
                                if qc == 0:
                                    wts.append((nrm, vbe))
                            E(ph, "tensor", lambda e, bk=bk, kk=kk, half=half, h=h, q0=q0: e.matmul(
                                  PF(bk + half), kT_sb[h][:, kk * 128:(kk + 1) * 128],
                                  qT_sb[h][:, q0:q0 + 512], start=True, stop=True),
                              waits=wts, inc=(tS if half == 1 else None))
                        pv = ctx_after[p - 3] if p >= 3 else tC_base
                        ev = E(ph, "scalar", lambda e, bk=bk, h=h, p=p: e.activation(
                                   out=ptb[p % 3][:], in_=pf[:, bk * 512: bk * 512 + 1024],
                                   func=AF.Exp, scale=SCALE_ATT, bias=bcol[:, h:h + 1]),
                               waits=[(tS, tS.n), (tC, pv)], inc=sP)
                        exp_vals.append(ev)
                        if p >= 2:
                            emit_ctx_pair(p - 2, h, exp_vals[p - 2])
                            ctx_after.append(tC.n)
                    emit_ctx_pair(NP - 2, h, exp_vals[NP - 2])
                    emit_ctx_pair(NP - 1, h, exp_vals[NP - 1])
                    for j in range(4):
                        rc = E(ph, "vector", lambda e, j=j: e.reciprocal(
                                  out=colbuf[:, 48 + j:49 + j], in_=CTXSUM(j)),
                              waits=([(tC, tC.n), (tT, tT.n)] if j == 0 else []), inc=vC)
                        E(ph, "vector", lambda e, j=j: e.tensor_scalar(
                              out=ctx_sb[:, j * 128:(j + 1) * 128],
                              in0=CTX(j)[:, 0:128],
                              scalar1=colbuf[:, 48 + j:49 + j], scalar2=None, op0=OP.mult),
                          waits=[(vC, rc)], inc=vC)
                    for j in range(4):
                        E(ph, "tensor", lambda e, j=j: e.transpose(
                              PB(1)[:, j * 128:(j + 1) * 128], ctx_sb[:, j * 128:(j + 1) * 128],
                              ident_sb[:]),
                          waits=([(vC, vC.n), (vE, vE.n)] if j == 0 else ()),
                          inc=(tT if j == 3 else None))
                    E(ph, "vector", lambda e, g2=g2, q0=q0: e.tensor_copy(
                          out=ctxaT[:, g2 * 2048 + q0: g2 * 2048 + q0 + 512], in_=PB(1)[:, 0:512]),
                      waits=[(tT, tT.n)], inc=vE)
        if b0:
            while b0_state["idx"] < len(b0_jobs):
                emit_b0_job()
            W(ph, "sync", [(spl, spl.n)])
        W(ph, "vector", [(vE, vE.n)])
        ph.run()
        stA.close()

        if "A2" in phases:
            stA2 = contextlib.ExitStack()
            sbA2 = lambda name, shp, dt: stA2.enter_context(nc.sbuf_tensor(name, shp, dt))
            wproj_sb = sbA2("wproj_sb", [128, ND * 2048], BF16)
            xcb = [sbA2(f"xcb{i}", [128, D], F32) for i in range(2)]
            x1b = [sbA2(f"x1b{i}", [128, D], F32) for i in range(2)]

            ph = Ph(nc)
            ldw2 = SM("ldw2"); ldxc = SM("ldxc")
            tO = SM("tOA2"); vR = SM("vRA2"); stx = SM("stxA2")
            for g in range(4):
                E(ph, "gpsimd", lambda e, g=g: e.dma_start(
                      out=wproj_sb[:].rearrange("p (a f) -> p a f", a=ND)[:, g * 4:(g + 1) * 4, :],
                      in_=wprojT.rearrange("(a p) f -> p a f", p=128)[:, g * 4:(g + 1) * 4, :]),
                  inc=ldw2, inck=16)
            W(ph, "tensor", [(ldw2, ldw2.n)])
            for tt in range(NT):
                bi = tt % 2
                vxc = E(ph, "sync", lambda e, tt=tt, bi=bi: e.dma_start(
                            out=xcb[bi][:], in_=xc[tt * 128:(tt + 1) * 128, :]),
                        waits=[(vR, max(0, vR.n - 4))], inc=ldxc, inck=16)
                for ec in range(4):
                    bk = ec % 2
                    wts = [(vR, max(0, vR.n - 1))]
                    for cd in range(ND):
                        E(ph, "tensor", lambda e, bk=bk, cd=cd, tt=tt, ec=ec: e.matmul(
                              PF(bk), ctxaT[:, cd * 2048 + tt * 128: cd * 2048 + tt * 128 + 128],
                              wproj_sb[:, cd * 2048 + ec * 512: cd * 2048 + ec * 512 + 512],
                              start=(cd == 0), stop=False),
                          waits=(wts if cd == 0 else ()))
                    E(ph, "tensor", lambda e, bk=bk, ec=ec: e.matmul(
                          PF(bk), onesr_sb[:], bprojb_sb[0:1, ec * 512:(ec + 1) * 512],
                          start=False, stop=True),
                      inc=tO)
                    awts = [(tO, tO.n)]
                    if ec == 0:
                        awts += [(ldxc, vxc), (stx, max(0, stx.n - 16))]
                    E(ph, "vector", lambda e, bk=bk, ec=ec, bi=bi: e.tensor_tensor(
                          out=x1b[bi][:, ec * 512:(ec + 1) * 512],
                          in0=xcb[bi][:, ec * 512:(ec + 1) * 512], in1=PF(bk), op=OP.add),
                      waits=awts, inc=vR)
                E(ph, "scalar", lambda e, tt=tt, bi=bi: e.dma_start(
                      out=xs[tt * 128:(tt + 1) * 128, :], in_=x1b[bi][:]),
                  waits=[(vR, vR.n)], inc=stx, inck=16)
            W(ph, "scalar", [(stx, stx.n)])
            ph.run()
            stA2.close()
        stAk.close()


    if "B0S" in phases and "B0" in phases:
        stB0 = contextlib.ExitStack()
        wib = [stB0.enter_context(nc.sbuf_tensor(f"wibs{i}", [128, II], mybir.dt.uint8)) for i in range(2)]
        wbb = [stB0.enter_context(nc.sbuf_tensor(f"wbbs{i}", [128, II], BF16)) for i in range(2)]
        ph = Ph(nc)
        ldq = SM("ldqB0s"); dqc = SM("dqcB0s"); spl = SM("splB0s")
        wg4 = wgscr.rearrange("(ii d p) f -> d p ii f", ii=NI, p=128)
        wu4 = wuscr.rearrange("(ii d p) f -> d p ii f", ii=NI, p=128)
        wd4 = wdscr.rearrange("(ec ii p) f -> ii p ec f", ec=4, p=128)
        jobs = ([("g", dt) for dt in range(ND)] + [("u", dt) for dt in range(ND)]
                + [("d", it) for it in range(NI)])
        spl_after = []
        for idx, (mat, t) in enumerate(jobs):
            bi = idx % 2
            ncols = II if mat in ("g", "u") else D
            src = {"g": gwT, "u": uwT, "d": dwT}[mat]
            vld = E(ph, "sync", lambda e, bi=bi, src=src, t=t, ncols=ncols: e.dma_start(
                        out=wib[bi][:, 0:ncols], in_=src[t * 128:(t + 1) * 128, :]),
                    waits=[(dqc, max(0, dqc.n - 1))], inc=ldq, inck=16)
            wts = [(ldq, vld)]
            if idx >= 2:
                wts.append((spl, spl_after[idx - 2]))
            vdq = E(ph, "vector", lambda e, bi=bi, ncols=ncols: e.tensor_scalar(
                        out=wbb[bi][:, 0:ncols], in0=wib[bi][:, 0:ncols],
                        scalar1=-128.0, scalar2=None, op0=OP.add),
                    waits=wts, inc=dqc)
            if mat in ("g", "u"):
                dst4 = wg4 if mat == "g" else wu4
                E(ph, "sync", lambda e, bi=bi, dst4=dst4, t=t: e.dma_start(
                      out=dst4[t],
                      in_=wbb[bi][:].rearrange("p (a f) -> p a f", a=NI)),
                  waits=[(dqc, vdq)], inc=spl, inck=16)
            else:
                E(ph, "sync", lambda e, bi=bi, t=t: e.dma_start(
                      out=wd4[t], in_=wbb[bi][:, 0:D].rearrange("p (a f) -> p a f", a=4)),
                  waits=[(dqc, vdq)], inc=spl, inck=16)
            spl_after.append(spl.n)
        W(ph, "sync", [(spl, spl.n)])
        ph.run()
        stB0.close()

    def phase_ffn(tag, src_ap, dst_ap):
        stF = contextlib.ExitStack()
        sbF = lambda name, shp, dt: stF.enter_context(nc.sbuf_tensor(name + tag, shp, dt))
        lnxF = sbF("lnxF", [128, ND * 512], BF16)
        dsbc_sb = sbF("dsbc_sb", [128, D], F32)
        xpool = [sbF(f"xp{i}", [128, D], F32) for i in range(4)]
        lnb = [sbF(f"lnbF{i}", [128, D], BF16) for i in range(2)]
        gwb = [sbF(f"gwb{i}", [128, ND * 128], BF16) for i in range(2)]
        uwb = [sbF(f"uwb{i}", [128, ND * 128], BF16) for i in range(2)]
        wdb = [sbF(f"wdb{i}", [128, 8 * 512], BF16) for i in range(2)]
        sgb = [sbF(f"sgb{i}", [128, 512], BF16) for i in range(2)]
        utb = [sbF(f"utb{i}", [128, 512], BF16) for i in range(2)]
        hact = sbF("hact", [128, NI * 512], BF16)
        dtmp = [sbF(f"dtmp{i}", [128, 512], F32) for i in range(2)]
        stg = [sbF(f"stg{i}", [128, 512], F32) for i in range(2)]

        ph = Ph(nc)
        smF = dict(ldx=SM("ldx" + tag), vst=SM("vst" + tag), sln=SM("sln" + tag),
                   apl=SM("apl" + tag), ttr=SM("ttr" + tag), tev=SM("tev" + tag))
        ldxF, vstF, slnF, aplF, ttrF, tevF = (smF[k] for k in ("ldx", "vst", "sln", "apl", "ttr", "tev"))
        ldgu = SM("ldgu" + tag); lddn = SM("lddn" + tag)
        tgu = SM("tgu" + tag); ssil = SM("ssil" + tag); vup = SM("vup" + tag); vh = SM("vh" + tag)
        tdn = SM("tdn" + tag); vdn = SM("vdn" + tag); stx2 = SM("stx2" + tag)

        ldds = SM("ldds" + tag)
        E(ph, "sync", lambda e: e.dma_start(out=dsbc_sb[:], in_=ds_bc[:]), inc=ldds, inck=16)
        W(ph, "vector", [(ldds, ldds.n)])

        lnxF3 = lnxF[:].rearrange("p (d f) -> p d f", d=ND)
        wgr = wgscr.rearrange("(a p) f -> a p f", p=128)
        wur = wuscr.rearrange("(a p) f -> a p f", p=128)
        wdr = wdscr.rearrange("(a p) f -> a p f", p=128)

        for u in range(4):
            for t4 in range(4):
                tg = u * 4 + t4
                vld = E(ph, "sync", lambda e, t4=t4, tg=tg: e.dma_start(
                            out=xpool[t4][:], in_=src_ap[tg * 128:(tg + 1) * 128, :]),
                        waits=[(vdn, vdn.n)], inc=ldxF, inck=16)
                for j in range(4):
                    E(ph, "vector", lambda e, j=j, t4=t4: e.bn_stats(statb[:, j * 6:j * 6 + 6],
                                                                     xpool[t4][:, j * 512:(j + 1) * 512]),
                      waits=([(ldxF, vld)] if j == 0 else ()), inc=(vstF if j == 3 else None))
                vagg = E(ph, "vector", lambda e: e.bn_aggr(statb[:, 24:26], statb[:, 0:24]),
                         waits=[(vstF, vstF.n)], inc=vstF)
                E(ph, "vector", lambda e: e.tensor_scalar(out=statb[:, 26:27], in0=statb[:, 24:25],
                                                          scalar1=-1.0, scalar2=None, op0=OP.mult),
                  waits=[(vstF, vagg)])
                vve = E(ph, "vector", lambda e: e.tensor_scalar(out=statb[:, 27:28], in0=statb[:, 25:26],
                                                                scalar1=LN_EPS, scalar2=None, op0=OP.add),
                        inc=vstF)
                s1 = E(ph, "scalar", lambda e: e.activation(out=statb[:, 28:29], in_=statb[:, 27:28], func=AF.Ln),
                       waits=[(vstF, vve)], inc=slnF)
                s2 = E(ph, "scalar", lambda e: e.activation(out=statb[:, 29:30], in_=statb[:, 28:29],
                                                            func=AF.Exp, scale=-0.5),
                       waits=[(slnF, s1)], inc=slnF)
                vn = E(ph, "vector", lambda e: e.tensor_tensor(out=statb[:, 30:31], in0=statb[:, 26:27],
                                                               in1=statb[:, 29:30], op=OP.mult),
                       waits=[(slnF, s2)], inc=vstF)
                va = E(ph, "vector", lambda e, t4=t4, lb=lnb[t4 % 2]: e.tensor_scalar(
                           out=lb[:], in0=xpool[t4][:], scalar1=statb[:, 29:30], scalar2=statb[:, 30:31],
                           op0=OP.mult, op1=OP.add),
                       waits=[(vstF, vn), (ttrF, ttrF.n)], inc=aplF)
                for h in range(2):
                    wts = ([(aplF, va)] if h == 0 else []) + [(tevF, tevF.n)]
                    for j in range(8):
                        dtile = h * 8 + j
                        E(ph, "tensor", lambda e, h=h, j=j, dtile=dtile, lb=lnb[t4 % 2]: e.transpose(
                              PB(h)[:, j * 128:(j + 1) * 128], lb[:, dtile * 128:(dtile + 1) * 128],
                              ident_sb[:]),
                          waits=(wts if j == 0 else ()), inc=(ttrF if j == 7 else None))
                    E(ph, "vector", lambda e, h=h, t4=t4: e.tensor_copy(
                          out=lnxF3[:, h * 8:(h + 1) * 8, t4 * 128:(t4 + 1) * 128],
                          in_=PB(h)[:].rearrange("p (j f) -> p j f", j=8)),
                      waits=[(ttrF, ttrF.n)], inc=tevF)
            for ii in range(NI):
                bi = ii % 2
                vlg = E(ph, "sync", lambda e, bi=bi, ii=ii: e.dma_start(
                            out=gwb[bi][:].rearrange("p (a f) -> p a f", a=ND),
                            in_=wgr[ii * ND:(ii + 1) * ND].rearrange("a p f -> p a f")),
                        waits=[(tgu, max(0, tgu.n - 3))], inc=ldgu, inck=16)
                vlu = E(ph, "gpsimd", lambda e, bi=bi, ii=ii: e.dma_start(
                            out=uwb[bi][:].rearrange("p (a f) -> p a f", a=ND),
                            in_=wur[ii * ND:(ii + 1) * ND].rearrange("a p f -> p a f")),
                        waits=[(tgu, max(0, tgu.n - 2))], inc=ldgu, inck=16)
                bg, bu = (0, 1) if bi == 0 else (2, 3)
                wts0 = [(ldgu, vlg)]
                if ii == 0:
                    wts0.append((tevF, tevF.n))
                    wts0.append((vdn, vdn.n))
                wts0.append((ssil, max(0, ssil.n - 1)))
                for dt in range(ND):
                    E(ph, "tensor", lambda e, bg=bg, bi=bi, dt=dt: e.matmul(
                          PF(bg), gwb[bi][:, dt * 128:(dt + 1) * 128], lnxF[:, dt * 512:(dt + 1) * 512],
                          start=(dt == 0), stop=(dt == ND - 1)),
                      waits=(wts0 if dt == 0 else ()), inc=(tgu if dt == ND - 1 else None))
                wts1 = [(ldgu, vlu), (vup, max(0, vup.n - 1))]
                for dt in range(ND):
                    E(ph, "tensor", lambda e, bu=bu, bi=bi, dt=dt: e.matmul(
                          PF(bu), uwb[bi][:, dt * 128:(dt + 1) * 128], lnxF[:, dt * 512:(dt + 1) * 512],
                          start=(dt == 0), stop=(dt == ND - 1)),
                      waits=(wts1 if dt == 0 else ()), inc=(tgu if dt == ND - 1 else None))
                E(ph, "scalar", lambda e, bg=bg, bi=bi, ii=ii: e.activation(
                      out=sgb[bi][:], in_=PF(bg), func=AF.Silu, scale=gs_sb[:, ii:ii + 1]),
                  waits=[(tgu, tgu.n - 1), (vh, max(0, vh.n - 1))], inc=ssil)
                vu = E(ph, "vector", lambda e, bu=bu, bi=bi, ii=ii: e.tensor_scalar(
                           out=utb[bi][:], in0=PF(bu), scalar1=us_sb[:, ii:ii + 1], scalar2=None, op0=OP.mult),
                       waits=[(tgu, tgu.n), (vh, max(0, vh.n - 1))], inc=vup)
                E(ph, "vector", lambda e, bi=bi, ii=ii: e.tensor_tensor(
                      out=hact[:, ii * 512:(ii + 1) * 512], in0=sgb[bi][:], in1=utb[bi][:], op=OP.mult),
                  waits=[(vup, vu), (ssil, ssil.n)], inc=vh)
            for ec in range(4):
                for ig in range(8):
                    bi = ig % 2
                    vld2 = E(ph, "sync", lambda e, bi=bi, ec=ec, ig=ig: e.dma_start(
                                 out=wdb[bi][:].rearrange("p (a f) -> p a f", a=8),
                                 in_=wdr[ec * NI + ig * 8: ec * NI + (ig + 1) * 8].rearrange("a p f -> p a f")),
                             waits=[(tdn, max(0, tdn.n - 1))], inc=lddn, inck=16)
                    for il in range(8):
                        ii = ig * 8 + il
                        for t4 in range(4):
                            wts = []
                            if ii == 0 and t4 == 0:
                                wts = [(vh, vh.n), (ssil, ssil.n), (vup, vup.n), (lddn, vld2)]
                                if ec > 0:
                                    wts.append((vdn, vdn.n))
                            elif il == 0 and t4 == 0:
                                wts = [(lddn, vld2)]
                            E(ph, "tensor", lambda e, t4=t4, ii=ii, il=il, bi=bi: e.matmul(
                                  PF(t4), hact[:, ii * 512 + t4 * 128: ii * 512 + t4 * 128 + 128],
                                  wdb[bi][:, il * 512:(il + 1) * 512],
                                  start=(ii == 0), stop=(ii == NI - 1)),
                              waits=wts, inc=(tdn if (il == 7 and t4 == 3) else None))
                for t4 in range(4):
                    bi2 = t4 % 2
                    vd1 = E(ph, "vector", lambda e, t4=t4, ec=ec, bi2=bi2: e.tensor_tensor(
                                out=dtmp[bi2][:], in0=PF(t4), in1=dsbc_sb[:, ec * 512:(ec + 1) * 512],
                                op=OP.mult),
                            waits=([(tdn, tdn.n)] if t4 == 0 else []),
                            inc=vdn)
                    E(ph, "vector", lambda e, t4=t4, ec=ec, bi2=bi2: e.tensor_tensor(
                          out=stg[bi2][:], in0=xpool[t4][:, ec * 512:(ec + 1) * 512], in1=dtmp[bi2][:],
                          op=OP.add),
                      waits=[(vdn, vd1), (stx2, max(0, stx2.n - 16))], inc=vdn)
                    E(ph, "scalar", lambda e, t4=t4, ec=ec, u=u, bi2=bi2: e.dma_start(
                          out=dst_ap[(u * 4 + t4) * 128:(u * 4 + t4) * 128 + 128, ec * 512:(ec + 1) * 512],
                          in_=stg[bi2][:]),
                      waits=[(vdn, vdn.n)], inc=stx2, inck=16)
        W(ph, "scalar", [(stx2, stx2.n)])
        ph.run()
        stF.close()

    if "B" in phases:
        phase_ffn("B", xs, xs)

    if "C" in phases:
        stC = contextlib.ExitStack()
        sbC = lambda name, shp, dt: stC.enter_context(nc.sbuf_tensor(name, shp, dt))
        lnx2T = sbC("lnx2T", [128, ND * S], BF16)
        masks_sb = sbC("masks_sb", [128, 8 * 512], BF16)
        ropec_sb = sbC("ropec_sb", [128, NT * 64], F32)
        ropes_sb = sbC("ropes_sb", [128, NT * 64], F32)
        qwb = [sbC(f"qwbC{i}", [128, ND * 512], BF16) for i in range(2)]
        kT_all = sbC("kT_all", [128, KVH * S], BF16)
        vaug4 = sbC("vaug4", [128, KVH * NT * 132], BF16)
        qT_grp = sbC("qT_grp", [128, 4 * S], BF16)
        ktm = [sbC(f"ktmC{i}", [128, 512], F32) for i in range(3)]
        robuf = [sbC(f"robufC{i}", [128, 512], BF16) for i in range(4)]
        sqd = sbC("sqdC", [128, 256], BF16)
        rtmp = [sbC(f"rtmpC{i}", [128, 64], F32) for i in range(6)]
        ptb2 = [sbC(f"ptbC{i}", [128, 1024], BF16) for i in range(3)]
        ctx_sb2 = sbC("ctx_sbC", [128, 512], BF16)
        ctmp = [sbC(f"ctmpC{i}", [128, 512], BF16) for i in range(2)]
        xbC0 = sbC("xbC0", [128, D], F32)
        lnbC0 = sbC("lnbC0", [128, D], BF16)
        xbC = [xbC0, xbC0]
        lnbC = [lnbC0, lnbC0]

        ph = Ph(nc)
        smC = dict(ldx=SM("ldxC"), vst=SM("vstC"), sln=SM("slnC"), apl=SM("aplC"),
                   ttr=SM("ttrC"), tev=SM("tevC"))
        ldxC, vstC, slnC, aplC, ttrC, tevC = (smC[k] for k in ("ldx", "vst", "sln", "apl", "ttr", "tev"))
        ldwq = SM("ldwqC")
        tqv = SM("tqvC")
        eqv = SM("eqvC")
        vro = SM("vroC")
        sroC = SM("sroC")
        tS2 = SM("tS2C"); sP2 = SM("sP2C"); vM2 = SM("vM2C"); tC2 = SM("tC2C")
        vC2 = SM("vC2C"); tT2 = SM("tT2C"); vE2 = SM("vE2C"); stc = SM("stcC")

        lnx2T3 = lnx2T[:].rearrange("p (d f) -> p d f", d=ND)
        va43 = vaug4[:].rearrange("p (j a f) -> p j a f", j=KVH, a=NT)
        masks3 = masks_sb[:].rearrange("p (a f) -> p a f", a=8)

        ldcc = SM("ldccC")
        E(ph, "sync", lambda e: e.dma_start(out=masks_sb[:].rearrange("p (a f) -> p a f", a=8),
                                            in_=maskb.rearrange("(a p) f -> p a f", p=128)),
          inc=ldcc, inck=16)
        E(ph, "sync", lambda e: e.dma_start(out=ropec_sb[:].rearrange("p (a f) -> p a f", a=NT),
                                            in_=ropec.rearrange("(a p) f -> p a f", p=128)),
          inc=ldcc, inck=16)
        E(ph, "sync", lambda e: e.dma_start(out=ropes_sb[:].rearrange("p (a f) -> p a f", a=NT),
                                            in_=ropes.rearrange("(a p) f -> p a f", p=128)),
          inc=ldcc, inck=16)
        W(ph, "vector", [(ldcc, ldcc.n)])

        sab = E(ph, "scalar", lambda e: e.activation(out=colbuf[:, 20:36], in_=qg_sb[:], func=AF.Abs),
                inc=sroC)
        E(ph, "vector", lambda e: e.tensor_scalar(
              out=bcol[:, 2:18], in0=colbuf[:, 20:36],
              scalar1=float(-1.005 * 128.0 * SCALE_ATT), scalar2=None, op0=OP.mult),
          waits=[(sroC, sab)], inc=vro)

        for tg in range(NT):
            bi = tg % 2
            emit_ln_tile(ph, smC, xs[tg * 128:(tg + 1) * 128, :], xbC[bi], lnbC[bi],
                         lambda h, tg=tg: lnx2T3[:, h * 8:(h + 1) * 8, tg * 128:(tg + 1) * 128],
                         nbuf=1)

        E(ph, "gpsimd", lambda e: e.dma_start(
              out=qwb[0][:].rearrange("p (a f) -> p a f", a=ND),
              in_=wkT.rearrange("(a p) f -> p a f", p=128)), inc=ldwq, inck=16)
        E(ph, "gpsimd", lambda e: e.dma_start(
              out=qwb[1][:].rearrange("p (a f) -> p a f", a=ND),
              in_=wvT.rearrange("(a p) f -> p a f", p=128)), inc=ldwq, inck=16)

        def emit_rms_rope(tt, src_f32, gain_slice, dstT, dst_off, kvh_mode):
            for j in range(4):
                E(ph, "vector", lambda e, j=j: e.tensor_tensor(
                      out=sqd[:, (j % 2) * 128:(j % 2) * 128 + 128],
                      in0=src_f32[:, j * 128:(j + 1) * 128],
                      in1=src_f32[:, j * 128:(j + 1) * 128], op=OP.mult))
                if j >= 1:
                    E(ph, "vector", lambda e, j=j: e.reduce_sum(
                          colbuf[:, 60 + j - 1:61 + j - 1],
                          sqd[:, ((j - 1) % 2) * 128:((j - 1) % 2) * 128 + 128], axis=AX.X))
            E(ph, "vector", lambda e: e.reduce_sum(colbuf[:, 63:64], sqd[:, 128:256], axis=AX.X),
              inc=vro)
            vme = E(ph, "vector", lambda e: e.tensor_scalar(
                        out=colbuf[:, 52:56], in0=colbuf[:, 60:64],
                        scalar1=1.0 / 128.0, scalar2=RMS_EPS, op0=OP.mult, op1=OP.add),
                    waits=[(vro, vro.n)], inc=vro)
            l1 = E(ph, "scalar", lambda e: e.activation(out=colbuf[:, 56:60], in_=colbuf[:, 52:56], func=AF.Ln),
                   waits=[(vro, vme)], inc=sroC)
            l2 = E(ph, "scalar", lambda e: e.activation(out=colbuf[:, 52:56], in_=colbuf[:, 56:60],
                                                        func=AF.Exp, scale=-0.5),
                   waits=[(sroC, l1)], inc=sroC)
            if gain_slice is not None:
                vg = E(ph, "vector", lambda e: e.tensor_tensor(out=colbuf[:, 56:60], in0=colbuf[:, 52:56],
                                                               in1=gain_slice, op=OP.mult),
                       waits=[(sroC, l2)], inc=vro)
                rcol = colbuf[:, 56:60]
                rwait = (vro, vg)
            else:
                rcol = colbuf[:, 52:56]
                rwait = (sroC, l2)
            rob = robuf[tt % 4]
            ct = ropec_sb[:, tt * 64:(tt + 1) * 64]
            sn = ropes_sb[:, tt * 64:(tt + 1) * 64]
            for jp in (0, 2):
                for j in (jp, jp + 1):
                    x1 = src_f32[:, j * 128:j * 128 + 64]
                    x2 = src_f32[:, j * 128 + 64:(j + 1) * 128]
                    q_ = 3 * (j - jp)
                    E(ph, "vector", lambda e, x1=x1, ct=ct, q_=q_: e.tensor_tensor(
                          out=rtmp[q_][:], in0=x1, in1=ct, op=OP.mult),
                      waits=([rwait, (ttrC, ttrC.n)] if (j == jp and jp == 0) else ()))
                    E(ph, "vector", lambda e, x2=x2, sn=sn, q_=q_: e.tensor_tensor(
                          out=rtmp[q_ + 1][:], in0=x2, in1=sn, op=OP.mult))
                    E(ph, "vector", lambda e, x1=x1, sn=sn, q_=q_: e.tensor_tensor(
                          out=rtmp[q_ + 2][:], in0=x1, in1=sn, op=OP.mult))
                for j in (jp, jp + 1):
                    q_ = 3 * (j - jp)
                    x2 = src_f32[:, j * 128 + 64:(j + 1) * 128]
                    E(ph, "vector", lambda e, q_=q_: e.tensor_tensor(
                          out=rtmp[q_][:], in0=rtmp[q_][:], in1=rtmp[q_ + 1][:], op=OP.add))
                    E(ph, "vector", lambda e, x2=x2, ct=ct, q_=q_: e.tensor_tensor(
                          out=rtmp[q_ + 1][:], in0=x2, in1=ct, op=OP.mult))
                E(ph, "vector", lambda e: e.tensor_tensor(
                      out=rtmp[1][:], in0=rtmp[1][:], in1=rtmp[2][:], op=OP.subtract))
                E(ph, "vector", lambda e, rob=rob, jp=jp, sc=rcol[:, jp:jp + 1]: e.tensor_scalar(
                      out=rob[:, jp * 128:jp * 128 + 64], in0=rtmp[0][:],
                      scalar1=sc, scalar2=None, op0=OP.mult))
                E(ph, "vector", lambda e: e.tensor_tensor(
                      out=rtmp[4][:], in0=rtmp[4][:], in1=rtmp[5][:], op=OP.subtract))
                E(ph, "vector", lambda e, rob=rob, jp=jp, sc=rcol[:, jp + 1:jp + 2]: e.tensor_scalar(
                      out=rob[:, (jp + 1) * 128:(jp + 1) * 128 + 64], in0=rtmp[3][:],
                      scalar1=sc, scalar2=None, op0=OP.mult))
                E(ph, "vector", lambda e, rob=rob, jp=jp, sc=rcol[:, jp:jp + 1]: e.tensor_scalar(
                      out=rob[:, jp * 128 + 64:jp * 128 + 128], in0=rtmp[1][:],
                      scalar1=sc, scalar2=None, op0=OP.mult))
                E(ph, "vector", lambda e, rob=rob, jp=jp, sc=rcol[:, jp + 1:jp + 2]: e.tensor_scalar(
                      out=rob[:, (jp + 1) * 128 + 64:(jp + 2) * 128], in0=rtmp[4][:],
                      scalar1=sc, scalar2=None, op0=OP.mult),
                  inc=(vro if jp == 2 else None))
            return vro.n

        def emit_rope_T(tt, dstT, rope_val):
            rob = robuf[tt % 4]
            for j in range(4):
                E(ph, "tensor", lambda e, j=j, rob=rob: e.transpose(
                      PB(1)[:, j * 128:(j + 1) * 128], rob[:, j * 128:(j + 1) * 128], ident_sb[:]),
                  waits=([(vro, rope_val), (tevC, tevC.n)] if j == 0 else ()),
                  inc=(ttrC if j == 3 else None))
            E(ph, "vector", lambda e, dstT=dstT, tt=tt: e.tensor_copy(
                  out=dstT[:].rearrange("p (j f) -> p j f", j=4)[:, :, tt * 128:(tt + 1) * 128],
                  in_=pb[:, 1024:1536].rearrange("p (j f) -> p j f", j=4)),
              waits=[(ttrC, ttrC.n)], inc=tevC)

        rope_vals = {}
        for tt in range(NT):
            bk = tt % 2
            wts = [(eqv, max(0, eqv.n - 1))]
            if tt == 0:
                wts += [(tevC, tevC.n), (ldwq, 16)]
            for dt in range(ND):
                E(ph, "tensor", lambda e, bk=bk, dt=dt, tt=tt: e.matmul(
                      PF(bk), lnx2T3[:, dt, tt * 128:(tt + 1) * 128],
                      qwb[0][:, dt * 512:(dt + 1) * 512],
                      start=(dt == 0), stop=(dt == ND - 1)),
                  waits=(wts if dt == 0 else ()), inc=(tqv if dt == ND - 1 else None))
            if tt >= 1:
                emit_rope_T(tt - 1, kT_all, rope_vals[tt - 1])
            ek = E(ph, "scalar", lambda e, bk=bk, tt=tt: e.copy(out=ktm[tt % 3][:], in_=PF(bk)),
                   waits=[(tqv, tqv.n), (vro, rope_vals.get(tt - 3, 0))], inc=eqv)
            W(ph, "vector", [(eqv, ek)])
            rope_vals[tt] = emit_rms_rope(tt, ktm[tt % 3], None, kT_all, 0, True)
            wts = [(eqv, max(0, eqv.n - 1)), (ldwq, 32)]
            for dt in range(ND):
                E(ph, "tensor", lambda e, bk=bk, dt=dt, tt=tt: e.matmul(
                      PF(2 + bk), lnx2T3[:, dt, tt * 128:(tt + 1) * 128],
                      qwb[1][:, dt * 512:(dt + 1) * 512],
                      start=(dt == 0), stop=(dt == ND - 1)),
                  waits=(wts if dt == 0 else ()), inc=(tqv if dt == ND - 1 else None))
            E(ph, "scalar", lambda e, bk=bk, tt=tt: e.copy(
                  out=va43[:, :, tt, 0:128],
                  in_=PF(2 + bk).rearrange("p (j f) -> p j f", j=4)),
              waits=[(tqv, tqv.n)], inc=eqv)
        emit_rope_T(NT - 1, kT_all, rope_vals[NT - 1])
        E(ph, "vector", lambda e: e.memset(va43[:, :, :, 128:129], 1.0),
          waits=[(eqv, eqv.n)], inc=tevC)

        for ec in range(4):
            bi = ec % 2
            vlq = E(ph, "gpsimd", lambda e, ec=ec, bi=bi: e.dma_start(
                        out=qwb[bi][:].rearrange("p (a f) -> p a f", a=ND),
                        in_=wqT[:, ec * 512:(ec + 1) * 512].rearrange("(a p) f -> p a f", p=128)),
                    waits=[(tqv, tqv.n)], inc=ldwq, inck=16)
            rope_vals_q = {}
            ec_base = vro.n
            for tt in range(NT):
                bk = tt % 2
                wts = [(eqv, max(0, eqv.n - 1)), (ldwq, vlq)]
                for dt in range(ND):
                    E(ph, "tensor", lambda e, bk=bk, dt=dt, tt=tt, bi=bi: e.matmul(
                          PF(bk), lnx2T3[:, dt, tt * 128:(tt + 1) * 128],
                          qwb[bi][:, dt * 512:(dt + 1) * 512],
                          start=(dt == 0), stop=(dt == ND - 1)),
                      waits=(wts if dt == 0 else ()), inc=(tqv if dt == ND - 1 else None))
                if tt >= 2:
                    emit_rope_T(tt - 2, qT_grp, rope_vals_q[tt - 2])
                eq = E(ph, "scalar", lambda e, bk=bk, tt=tt: e.copy(out=ktm[tt % 3][:], in_=PF(bk)),
                       waits=[(tqv, tqv.n), (vro, rope_vals_q.get(tt - 3, ec_base))], inc=eqv)
                W(ph, "vector", [(eqv, eq)])
                rope_vals_q[tt] = emit_rms_rope(tt, ktm[tt % 3], qg_sb[:, ec * 4:(ec + 1) * 4],
                                                qT_grp, 0, False)
            emit_rope_T(NT - 2, qT_grp, rope_vals_q[NT - 2])
            emit_rope_T(NT - 1, qT_grp, rope_vals_q[NT - 1])
            CTX = lambda j: pf[:, (4 + j // 2) * 512 + (j % 2) * 130:
                               (4 + j // 2) * 512 + (j % 2) * 130 + 129]
            CTXSUM = lambda j: pf[:, (4 + j // 2) * 512 + (j % 2) * 130 + 128:
                                  (4 + j // 2) * 512 + (j % 2) * 130 + 129]
            for hq in range(4):
                head = ec * 4 + hq
                for qc in range(4):
                    q0 = qc * 512
                    kklo = max(0, (qc - 1) * 4)
                    kkhi = qc * 4 + 3
                    pairs = [(kklo + 2 * i, kklo + 2 * i + 1) for i in range((kkhi + 1 - kklo) // 2)]
                    npair = len(pairs)

                    def emit_ctx_pair_b(pi, mask_val, ec=ec, kklo=kklo, kkhi=kkhi, pairs=pairs):
                        for half in range(2):
                            kk = pairs[pi][half]
                            for j in range(4):
                                E(ph, "tensor", lambda e, j=j, half=half, kk=kk, pi=pi, ec=ec: e.matmul(
                                      CTX(j), ptb2[pi % 3][:, half * 512 + j * 128: half * 512 + (j + 1) * 128],
                                      vaug4[:, ec * (NT * 132) + kk * 132: ec * (NT * 132) + kk * 132 + 129],
                                      start=(kk == kklo and j % 2 == 0), stop=(kk == kkhi)),
                                  waits=([(vM2, mask_val)] if (half == 0 and j == 0) else ()),
                                  inc=(tC2 if (half == 1 and j == 3) else None))

                    mask_vals = []
                    ctxb_after = []
                    chunk_base = tC2.n
                    for pi in range(npair):
                        bk2 = 2 * (pi % 2)
                        for half in range(2):
                            kk = pairs[pi][half]
                            wts = []
                            if half == 0:
                                wts.append((sP2, max(0, sP2.n - 1)))
                            if pi == 0 and half == 0:
                                wts.append((vC2, vC2.n))
                                wts.append((eqv, eqv.n))
                                if ec == 0 and hq == 0 and qc == 0:
                                    wts.append((tevC, tevC.n))
                            E(ph, "tensor", lambda e, bk2=bk2, kk=kk, half=half, ec=ec, hq=hq, q0=q0: e.matmul(
                                  PF(bk2 + half), kT_all[:, ec * 2048 + kk * 128: ec * 2048 + (kk + 1) * 128],
                                  qT_grp[:, hq * 2048 + q0: hq * 2048 + q0 + 512],
                                  start=True, stop=True),
                              waits=wts, inc=(tS2 if half == 1 else None))
                        pvb = ctxb_after[pi - 3] if pi >= 3 else chunk_base
                        E(ph, "scalar", lambda e, bk2=bk2, head=head, pi=pi: e.activation(
                              out=ptb2[pi % 3][:], in_=pf[:, bk2 * 512: bk2 * 512 + 1024],
                              func=AF.Exp, scale=SCALE_ATT, bias=bcol[:, 2 + head:3 + head]),
                          waits=[(tS2, tS2.n), (tC2, pvb)], inc=sP2)
                        for half in range(2):
                            kk = pairs[pi][half]
                            midx = (512 - (q0 - kk * 128)) // 128
                            mv = E(ph, "vector", lambda e, pi=pi, half=half, midx=midx: e.tensor_tensor(
                                       out=ptb2[pi % 3][:, half * 512:(half + 1) * 512],
                                       in0=ptb2[pi % 3][:, half * 512:(half + 1) * 512],
                                       in1=masks3[:, midx, :], op=OP.mult),
                                   waits=([(sP2, sP2.n)] if half == 0 else ()), inc=vM2)
                        mask_vals.append(mv)
                        if pi >= 2:
                            emit_ctx_pair_b(pi - 2, mask_vals[pi - 2])
                            ctxb_after.append(tC2.n)
                    if npair >= 2:
                        emit_ctx_pair_b(npair - 2, mask_vals[npair - 2])
                    emit_ctx_pair_b(npair - 1, mask_vals[npair - 1])
                    for j in range(4):
                        rc = E(ph, "vector", lambda e, j=j: e.reciprocal(
                                  out=colbuf[:, 48 + j:49 + j], in_=CTXSUM(j)),
                              waits=([(tC2, tC2.n), (tT2, tT2.n)] if j == 0 else []), inc=vC2)
                        E(ph, "vector", lambda e, j=j: e.tensor_scalar(
                              out=ctx_sb2[:, j * 128:(j + 1) * 128],
                              in0=CTX(j)[:, 0:128],
                              scalar1=colbuf[:, 48 + j:49 + j], scalar2=None, op0=OP.mult),
                          waits=[(vC2, rc)], inc=vC2)
                    for j in range(4):
                        E(ph, "tensor", lambda e, j=j: e.transpose(
                              PB(0)[:, j * 128:(j + 1) * 128], ctx_sb2[:, j * 128:(j + 1) * 128],
                              ident_sb[:]),
                          waits=([(vC2, vC2.n), (vE2, vE2.n)] if j == 0 else ()),
                          inc=(tT2 if j == 3 else None))
                    ce = E(ph, "vector", lambda e, qc=qc: e.tensor_copy(
                               out=ctmp[qc % 2][:], in_=pb[:, 0:512]),
                           waits=[(tT2, tT2.n), (stc, max(0, stc.n - 16))], inc=vE2)
                    E(ph, "scalar", lambda e, head=head, q0=q0, qc=qc: e.dma_start(
                          out=ctxb_scr[head * 128:(head + 1) * 128, q0:q0 + 512], in_=ctmp[qc % 2][:]),
                      waits=[(vE2, ce)], inc=stc, inck=16)
        W(ph, "scalar", [(stc, stc.n)])
        ph.run()
        stC.close()

        if "C2" in phases:
            stC2 = contextlib.ExitStack()
            sbC2 = lambda name, shp, dt: stC2.enter_context(nc.sbuf_tensor(name, shp, dt))
            cxT_sb = sbC2("cxT_sb", [128, NH * 2048], BF16)
            wo_sb = sbC2("wo_sb", [128, ND * 2048], BF16)
            xcb2 = [sbC2(f"xcb2{i}", [128, D], F32) for i in range(2)]
            x3b = [sbC2(f"x3b{i}", [128, D], F32) for i in range(2)]

            ph = Ph(nc)
            ldw3 = SM("ldw3"); ldxc2 = SM("ldxc2")
            tO2 = SM("tOC2"); vR2 = SM("vRC2"); stx3 = SM("stxC2")
            for g in range(4):
                E(ph, "gpsimd", lambda e, g=g: e.dma_start(
                      out=wo_sb[:].rearrange("p (a f) -> p a f", a=ND)[:, g * 4:(g + 1) * 4, :],
                      in_=woT.rearrange("(a p) f -> p a f", p=128)[:, g * 4:(g + 1) * 4, :]),
                  inc=ldw3, inck=16)
            for g in range(4):
                E(ph, "sync", lambda e, g=g: e.dma_start(
                      out=cxT_sb[:].rearrange("p (a f) -> p a f", a=NH)[:, g * 4:(g + 1) * 4, :],
                      in_=ctxb_scr.rearrange("(a p) f -> p a f", p=128)[:, g * 4:(g + 1) * 4, :]),
                  inc=ldw3, inck=16)
            W(ph, "tensor", [(ldw3, ldw3.n)])
            for tt in range(NT):
                bi = tt % 2
                vxc = E(ph, "sync", lambda e, tt=tt, bi=bi: e.dma_start(
                            out=xcb2[bi][:], in_=xs[tt * 128:(tt + 1) * 128, :]),
                        waits=[(vR2, max(0, vR2.n - 4))], inc=ldxc2, inck=16)
                for ec in range(4):
                    bk = ec % 2
                    wts = [(vR2, max(0, vR2.n - 1))]
                    for cd in range(ND):
                        E(ph, "tensor", lambda e, bk=bk, cd=cd, tt=tt, ec=ec: e.matmul(
                              PF(bk), cxT_sb[:, cd * 2048 + tt * 128: cd * 2048 + tt * 128 + 128],
                              wo_sb[:, cd * 2048 + ec * 512: cd * 2048 + ec * 512 + 512],
                              start=(cd == 0), stop=(cd == ND - 1)),
                          waits=(wts if cd == 0 else ()), inc=(tO2 if cd == ND - 1 else None))
                    awts = [(tO2, tO2.n)]
                    if ec == 0:
                        awts += [(ldxc2, vxc), (stx3, max(0, stx3.n - 16))]
                    E(ph, "vector", lambda e, bk=bk, ec=ec, bi=bi: e.tensor_tensor(
                          out=x3b[bi][:, ec * 512:(ec + 1) * 512],
                          in0=xcb2[bi][:, ec * 512:(ec + 1) * 512], in1=PF(bk), op=OP.add),
                      waits=awts, inc=vR2)
                E(ph, "scalar", lambda e, tt=tt, bi=bi: e.dma_start(
                      out=xs[tt * 128:(tt + 1) * 128, :], in_=x3b[bi][:]),
                  waits=[(vR2, vR2.n)], inc=stx3, inck=16)
            W(ph, "scalar", [(stx3, stx3.n)])
            ph.run()
            stC2.close()

    if "D" in phases:
        phase_ffn("D", xs, y)

    if "DBG" in phases:
        stDb = contextlib.ExitStack()
        db = stDb.enter_context(nc.sbuf_tensor("dbgb", [128, 512], BF16))
        dbf = stDb.enter_context(nc.sbuf_tensor("dbgf", [128, 512], F32))
        ph = Ph(nc)
        ldd = SM("ldDBG"); vcd = SM("vcDBG"); std = SM("stDBG")
        for blk in range(4):
            for r in range(16):
                vl = E(ph, "sync", lambda e, blk=blk, r=r: e.dma_start(
                           out=db[:], in_=wdscr[blk * 8192 + r * 128: blk * 8192 + (r + 1) * 128, :]),
                       waits=[(vcd, vcd.n)], inc=ldd, inck=16)
                vc = E(ph, "vector", lambda e: e.tensor_copy(out=dbf[:], in_=db[:]),
                       waits=[(ldd, vl), (std, std.n)], inc=vcd)
                E(ph, "scalar", lambda e, blk=blk, r=r: e.dma_start(
                      out=y[r * 128:(r + 1) * 128, blk * 512:(blk + 1) * 512], in_=dbf[:]),
                  waits=[(vcd, vc)], inc=std, inck=16)
        W(ph, "scalar", [(std, std.n)])
        ph.run()
        stDb.close()

    if "E" in phases:
        stE = contextlib.ExitStack()
        ob = [stE.enter_context(nc.sbuf_tensor(f"ob{i}", [128, D], F32)) for i in range(2)]
        ph = Ph(nc)
        lde = SM("ldE"); ste = SM("stE")
        for tt in range(NT):
            bi = tt % 2
            vl = E(ph, "sync", lambda e, tt=tt, bi=bi: e.dma_start(
                       out=ob[bi][:], in_=xs[tt * 128:(tt + 1) * 128, :]),
                   waits=[(ste, max(0, ste.n - 16))], inc=lde, inck=16)
            E(ph, "sync", lambda e, tt=tt, bi=bi: e.dma_start(
                  out=y[tt * 128:(tt + 1) * 128, :], in_=ob[bi][:]),
              waits=[(lde, vl)], inc=ste, inck=16)
        W(ph, "sync", [(ste, ste.n)])
        ph.run()
        stE.close()

    st.close()
    return nc



def _host_prep(inputs):
    f32 = np.float32
    _ = inputs["L0_attn_a_key"], inputs["L0_attn_a_value"]
    x = np.asarray(inputs["L0_attn_a_query"], f32)
    Wqkv = np.asarray(inputs["L0_attn_a_Wqkv"], f32)
    Wproj = np.asarray(inputs["L0_attn_a_Wproj"], f32)
    bproj = np.asarray(inputs["L0_attn_a_bproj"], f32)
    Wq = np.asarray(inputs["L0_attn_b_Wq"], f32)
    Wk = np.asarray(inputs["L0_attn_b_Wk"], f32)
    Wv = np.asarray(inputs["L0_attn_b_Wv"], f32)
    Wo = np.asarray(inputs["L0_attn_b_Wo"], f32)
    qg = np.asarray(inputs["L0_attn_b_q_gain"], f32)
    gw = np.asarray(inputs["L0_ffn_gate_w"], np.int32)
    gs = np.asarray(inputs["L0_ffn_gate_scale"], f32)
    uw = np.asarray(inputs["L0_ffn_up_w"], np.int32)
    us = np.asarray(inputs["L0_ffn_up_scale"], f32)
    dw = np.asarray(inputs["L0_ffn_down_w"], np.int32)
    ds = np.asarray(inputs["L0_ffn_down_scale"], f32)

    shared = {}
    shared["xall"] = np.ascontiguousarray(x.reshape(B * S, D))
    shared["wprojT"] = np.ascontiguousarray(Wproj.T)
    shared["bprojT"] = np.ascontiguousarray(bproj[None, :])
    shared["wqT"] = np.ascontiguousarray(Wq.T)
    shared["wkT"] = np.ascontiguousarray(Wk.T)
    shared["wvT"] = np.ascontiguousarray(Wv.T)
    shared["woT"] = np.ascontiguousarray(Wo.T)
    shared["gwT"] = np.ascontiguousarray(gw.T.astype(np.uint8))
    shared["uwT"] = np.ascontiguousarray(uw.T.astype(np.uint8))
    shared["dwT"] = np.ascontiguousarray(dw.T.astype(np.uint8))
    shared["gs_col"] = np.ascontiguousarray(gs.reshape(NI, 128).T)
    shared["us_col"] = np.ascontiguousarray(us.reshape(NI, 128).T)
    shared["ds_bc"] = np.ascontiguousarray(np.broadcast_to(ds[None, :], (128, D)))
    shared["qg_bc"] = np.ascontiguousarray(np.broadcast_to(qg[None, :], (128, NH)))
    shared["identb"] = np.eye(128, dtype=ml_dtypes.bfloat16)
    inv_freq = 10000.0 ** (-np.arange(0, 64, dtype=np.float64) / 64.0)
    freqs = np.arange(S, dtype=np.float64)[:, None] * inv_freq[None, :]
    shared["ropec"] = np.cos(freqs).astype(f32)
    shared["ropes"] = np.sin(freqs).astype(f32)
    qi = np.arange(512)[None, :]
    ki = np.arange(128)[:, None]
    masks = []
    for r in (512, 384, 256, 128, 0, -128, -256, -384):
        dm = qi - ki + r
        masks.append(((dm >= 0) & (dm <= WIN - 1)).astype(ml_dtypes.bfloat16))
    shared["maskb"] = np.ascontiguousarray(np.concatenate(masks, 0))
    shared["ones_row"] = np.ones((1, 128), dtype=ml_dtypes.bfloat16)
    shared["ones_col"] = np.ones((128, 1), dtype=ml_dtypes.bfloat16)

    in_maps = []
    for c in range(B):
        mm = dict(shared)
        mm["xc"] = np.ascontiguousarray(x[c])
        rows = np.concatenate([Wqkv[c * 384:(c + 1) * 384], Wqkv[(c + 8) * 384:(c + 9) * 384]], 0)
        mm["wqkvT"] = np.ascontiguousarray(rows.T)
        in_maps.append(mm)
    return in_maps


def kernel(**inputs):
    in_maps = _host_prep(inputs)
    nc = _build(tuple(PHASES))
    res = run_bass_kernel_spmd(nc, in_maps, core_ids=list(range(B)))
    return np.stack([r["y"] for r in res.results], 0)


# version 17
# speedup vs baseline: 1.1249x; 1.1249x over previous
import contextlib
import numpy as np
import ml_dtypes
import concourse.bass as bass
import concourse.mybir as mybir
from concourse.bass_utils import run_bass_kernel_spmd

F32 = mybir.dt.float32
BF16 = mybir.dt.bfloat16
I32 = mybir.dt.int32
AF = mybir.ActivationFunctionType
OP = mybir.AluOpType
AX = mybir.AxisListType

B, S, D = 8, 2048, 2048
NH, HD, KVH = 16, 128, 4
WIN = 512
II = 8192
NT = S // 128
ND = D // 128
NI = II // 128
SCALE_ATT = float(1.0 / np.sqrt(128.0))
LN_EPS = 1e-5
RMS_EPS = 1.1920928955078125e-07

PHASES = ("A", "A2", "B0", "B0S", "B", "C", "C2", "D")

ENGS = ("sync", "tensor", "vector", "scalar", "gpsimd")


class Ph:
    def __init__(self, nc):
        self.nc = nc
        self.q = {e: [] for e in ENGS}

    def em(self, eng, fn):
        self.q[eng].append(fn)

    def run(self):
        with self.nc.Block() as block:
            for e in ENGS:
                ops = self.q[e]
                if ops:
                    getattr(block, e)(lambda eng, ops=ops: [f(eng) for f in ops])


class Sem:
    def __init__(self, stack, nc, name):
        self.h = stack.enter_context(nc.semaphore(name))
        self.n = 0


def E(ph, eng, build, waits=(), inc=None, inck=1):
    if inc is not None:
        inc.n += inck
    def f(e, build=build, waits=tuple(waits), inc=inc, inck=inck):
        for s_, v_ in waits:
            e.wait_ge(s_.h, v_)
        inst = build(e)
        if inc is not None:
            inst.then_inc(inc.h, inck)
    ph.em(eng, f)
    return inc.n if inc is not None else None


def W(ph, eng, waits):
    def f(e, waits=tuple(waits)):
        for s_, v_ in waits:
            e.wait_ge(s_.h, v_)
    ph.em(eng, f)


def _build(phases=PHASES):
    nc = bass.Bass("TRN2", target_bir_lowering=False)
    st = contextlib.ExitStack()

    xall = nc.dram_tensor("xall", [B * S, D], F32, kind="ExternalInput").ap()
    xc = nc.dram_tensor("xc", [S, D], F32, kind="ExternalInput").ap()
    wqkvT = nc.dram_tensor("wqkvT", [D, 768], F32, kind="ExternalInput").ap()
    wprojT = nc.dram_tensor("wprojT", [D, D], F32, kind="ExternalInput").ap()
    bprojT = nc.dram_tensor("bprojT", [1, D], F32, kind="ExternalInput").ap()
    wqT = nc.dram_tensor("wqT", [D, D], F32, kind="ExternalInput").ap()
    wkT = nc.dram_tensor("wkT", [D, 512], F32, kind="ExternalInput").ap()
    wvT = nc.dram_tensor("wvT", [D, 512], F32, kind="ExternalInput").ap()
    woT = nc.dram_tensor("woT", [D, D], F32, kind="ExternalInput").ap()
    U8 = mybir.dt.uint8
    gwT = nc.dram_tensor("gwT", [D, II], U8, kind="ExternalInput").ap()
    uwT = nc.dram_tensor("uwT", [D, II], U8, kind="ExternalInput").ap()
    dwT = nc.dram_tensor("dwT", [II, D], U8, kind="ExternalInput").ap()
    gs_col = nc.dram_tensor("gs_col", [128, NI], F32, kind="ExternalInput").ap()
    us_col = nc.dram_tensor("us_col", [128, NI], F32, kind="ExternalInput").ap()
    ds_bc = nc.dram_tensor("ds_bc", [128, D], F32, kind="ExternalInput").ap()
    qg_bc = nc.dram_tensor("qg_bc", [128, NH], F32, kind="ExternalInput").ap()
    identb = nc.dram_tensor("identb", [128, 128], BF16, kind="ExternalInput").ap()
    ropec = nc.dram_tensor("ropec", [S, 64], F32, kind="ExternalInput").ap()
    ropes = nc.dram_tensor("ropes", [S, 64], F32, kind="ExternalInput").ap()
    maskb = nc.dram_tensor("maskb", [8 * 128, 512], BF16, kind="ExternalInput").ap()
    ones_row = nc.dram_tensor("ones_row", [1, 128], BF16, kind="ExternalInput").ap()
    ones_col = nc.dram_tensor("ones_col", [128, 1], BF16, kind="ExternalInput").ap()

    y = nc.dram_tensor("y", [S, D], F32, kind="ExternalOutput").ap()

    xs = nc.dram_tensor("xs", [S, D], F32, kind="Internal").ap()
    wgscr = nc.dram_tensor("wgscr", [NI * ND * 128, 128], BF16, kind="Internal").ap()
    wuscr = nc.dram_tensor("wuscr", [NI * ND * 128, 128], BF16, kind="Internal").ap()
    wdscr = nc.dram_tensor("wdscr", [4 * NI * 128, 512], BF16, kind="Internal").ap()
    ctxb_scr = nc.dram_tensor("ctxb_scr", [NH * 128, S], BF16, kind="Internal").ap()

    sb = lambda name, shp, dt: st.enter_context(nc.sbuf_tensor(name, shp, dt))
    ident_sb = sb("ident_sb", [128, 128], BF16)
    gs_sb = sb("gs_sb", [128, NI], F32)
    us_sb = sb("us_sb", [128, NI], F32)
    qg_sb = sb("qg_sb", [128, NH], F32)
    onesr_sb = sb("onesr_sb", [1, 128], BF16)
    onesc_sb = sb("onesc_sb", [128, 1], BF16)
    bprojb_sb = sb("bprojb_sb", [1, D], BF16)
    colbuf = sb("colbuf", [128, 64], F32)
    rowbuf = sb("rowbuf", [1, 2048], F32)
    statb = sb("statb", [128, 128], F32)
    bcol = sb("bcol", [128, 18], F32)
    bb16 = sb("bb16", [1, 2], BF16)

    pf = st.enter_context(nc.psum_tensor("pf", [128, 3072], F32))
    pb = st.enter_context(nc.psum_tensor("pb", [128, 2048], BF16))
    PF = lambda i: pf[:, i * 512:(i + 1) * 512]
    PB = lambda h: pb[:, h * 1024:(h + 1) * 1024]

    sems = {}
    def SM(name):
        if name not in sems:
            sems[name] = Sem(st, nc, name)
        return sems[name]

    ph = Ph(nc)
    ldc = SM("ldc")
    def ld(dst, src, eng="sync"):
        E(ph, eng, lambda e, d=dst, s=src: e.dma_start(out=d, in_=s), inc=ldc, inck=16)
    ld(ident_sb[:], identb[:])
    ld(gs_sb[:], gs_col[:])
    ld(us_sb[:], us_col[:])
    ld(qg_sb[:], qg_bc[:])
    ld(onesr_sb[:], ones_row[:])
    ld(onesc_sb[:], ones_col[:])
    ld(bprojb_sb[:], bprojT[:], eng="gpsimd")
    W(ph, "sync", [(ldc, ldc.n)])
    W(ph, "gpsimd", [(ldc, ldc.n)])
    ph.run()

    def emit_ln_tile(ph, sm, xsrc_ap, xbuf, lnxbuf, dst3_fn, nbuf=2):
        ldx, vst, sln, apl, ttr, tev = (sm[k] for k in ("ldx", "vst", "sln", "apl", "ttr", "tev"))
        vld = E(ph, "sync", lambda e, d=xbuf, s=xsrc_ap: e.dma_start(out=d[:], in_=s),
                waits=[(apl, max(0, apl.n - (nbuf - 1)))], inc=ldx, inck=16)
        for j in range(4):
            E(ph, "vector", lambda e, j=j, xb=xbuf: e.bn_stats(statb[:, j * 6:j * 6 + 6],
                                                               xb[:, j * 512:(j + 1) * 512]),
              waits=([(ldx, vld)] if j == 0 else ()), inc=(vst if j == 3 else None))
        vagg = E(ph, "vector", lambda e: e.bn_aggr(statb[:, 24:26], statb[:, 0:24]),
                 waits=[(vst, vst.n)], inc=vst)
        E(ph, "vector", lambda e: e.tensor_scalar(out=statb[:, 26:27], in0=statb[:, 24:25],
                                                  scalar1=-1.0, scalar2=None, op0=OP.mult),
          waits=[(vst, vagg)])
        vve = E(ph, "vector", lambda e: e.tensor_scalar(out=statb[:, 27:28], in0=statb[:, 25:26],
                                                        scalar1=LN_EPS, scalar2=None, op0=OP.add),
                inc=vst)
        s1 = E(ph, "scalar", lambda e: e.activation(out=statb[:, 28:29], in_=statb[:, 27:28], func=AF.Ln),
               waits=[(vst, vve)], inc=sln)
        s2 = E(ph, "scalar", lambda e: e.activation(out=statb[:, 29:30], in_=statb[:, 28:29],
                                                    func=AF.Exp, scale=-0.5),
               waits=[(sln, s1)], inc=sln)
        vn = E(ph, "vector", lambda e: e.tensor_tensor(out=statb[:, 30:31], in0=statb[:, 26:27],
                                                       in1=statb[:, 29:30], op=OP.mult),
               waits=[(sln, s2)], inc=vst)
        va = E(ph, "vector", lambda e, xb=xbuf, lb=lnxbuf: e.tensor_scalar(
                   out=lb[:], in0=xb[:], scalar1=statb[:, 29:30], scalar2=statb[:, 30:31],
                   op0=OP.mult, op1=OP.add),
               waits=[(vst, vn), (ttr, ttr.n)], inc=apl)
        for h in range(2):
            wts = ([(apl, va)] if h == 0 else []) + [(tev, tev.n)]
            for j in range(8):
                dtile = h * 8 + j
                E(ph, "tensor", lambda e, h=h, j=j, dtile=dtile, lb=lnxbuf: e.transpose(
                      PB(h)[:, j * 128:(j + 1) * 128], lb[:, dtile * 128:(dtile + 1) * 128], ident_sb[:]),
                  waits=(wts if j == 0 else ()), inc=(ttr if j == 7 else None))
            E(ph, "vector", lambda e, h=h, dst3=dst3_fn(h): e.tensor_copy(
                  out=dst3, in_=PB(h)[:].rearrange("p (j f) -> p j f", j=8)),
              waits=[(ttr, ttr.n)], inc=tev)

    def emit_ln_chunk(ph, sm, src_aps, xbufs, lnxbufs, dst3_fns, apl_vals, ttr_vals):
        ldx, vst, sln, apl, ttr, tev = (sm[k] for k in ("ldx", "vst", "sln", "apl", "ttr", "tev"))
        dvals = []
        stat_vals = []
        for i in range(4):
            k = len(apl_vals) - 4 + i
            gw = [(apl, apl_vals[k])] if k >= 0 else []
            dvals.append(E(ph, "sync", lambda e, d=xbufs[i], s=src_aps[i]: e.dma_start(out=d[:], in_=s),
                           waits=gw, inc=ldx, inck=16))
        for i in range(4):
            for j in range(4):
                E(ph, "vector", lambda e, i=i, j=j, xb=xbufs[i]: e.bn_stats(
                      statb[:, i * 24 + j * 6:i * 24 + j * 6 + 6], xb[:, j * 512:(j + 1) * 512]),
                  waits=([(ldx, dvals[i])] if j == 0 else ()),
                  inc=(vst if j == 3 else None))
            stat_vals.append(vst.n)
        for i in range(4):
            E(ph, "vector", lambda e, i=i: e.bn_aggr(statb[:, 96 + 2 * i:98 + 2 * i],
                                                     statb[:, i * 24:i * 24 + 24]),
              waits=[(vst, stat_vals[i])], inc=(vst if i == 3 else None))
        agg = vst.n
        mv3 = statb[:, 96:104].rearrange("p (a f) -> p a f", f=2)
        E(ph, "vector", lambda e, mv3=mv3: e.tensor_scalar(
              out=statb[:, 116:120], in0=mv3[:, :, 0:1], scalar1=-1.0, scalar2=None, op0=OP.mult),
          waits=[(vst, agg)])
        vv = E(ph, "vector", lambda e, mv3=mv3: e.tensor_scalar(
                   out=statb[:, 104:108], in0=mv3[:, :, 1:2], scalar1=LN_EPS, scalar2=None, op0=OP.add),
               inc=vst)
        s1 = E(ph, "scalar", lambda e: e.activation(out=statb[:, 108:112], in_=statb[:, 104:108], func=AF.Ln),
               waits=[(vst, vv)], inc=sln)
        s2 = E(ph, "scalar", lambda e: e.activation(out=statb[:, 112:116], in_=statb[:, 108:112],
                                                    func=AF.Exp, scale=-0.5),
               waits=[(sln, s1)], inc=sln)
        vn = E(ph, "vector", lambda e: e.tensor_tensor(out=statb[:, 120:124], in0=statb[:, 116:120],
                                                       in1=statb[:, 112:116], op=OP.mult),
               waits=[(sln, s2)], inc=vst)
        for i in range(4):
            kb = len(ttr_vals) - 2
            tw = [(ttr, ttr_vals[kb])] if kb >= 0 else []
            va = E(ph, "scalar", lambda e, i=i, xb=xbufs[i], lb=lnxbufs[i % 2]: e.activation(
                       out=lb[:], in_=xb[:], func=AF.Identity, scale=statb[:, 112 + i:113 + i],
                       bias=statb[:, 120 + i:121 + i]),
                   waits=[(vst, vn)] + tw, inc=apl)
            apl_vals.append(va)
            for h in range(2):
                wts = ([(apl, va)] if h == 0 else []) + [(tev, tev.n)]
                for j in range(8):
                    dtile = h * 8 + j
                    E(ph, "tensor", lambda e, h=h, j=j, dtile=dtile, lb=lnxbufs[i % 2]: e.transpose(
                          PB(h)[:, j * 128:(j + 1) * 128], lb[:, dtile * 128:(dtile + 1) * 128],
                          ident_sb[:]),
                      waits=(wts if j == 0 else ()), inc=(ttr if j == 7 else None))
                eng = "scalar" if h == 0 else "vector"
                E(ph, eng, lambda e, h=h, dst3=dst3_fns[i](h): e.tensor_copy(
                      out=dst3, in_=PB(h)[:].rearrange("p (j f) -> p j f", j=8))
                  if h else e.copy(out=dst3, in_=PB(h)[:].rearrange("p (j f) -> p j f", j=8)),
                  waits=[(ttr, ttr.n)], inc=tev)
            ttr_vals.append(ttr.n)

    if "A" in phases:
        stAk = contextlib.ExitStack()
        ctxaT = stAk.enter_context(nc.sbuf_tensor("ctxaT", [128, 16 * 2048], BF16))

        stA = contextlib.ExitStack()
        sbA = lambda name, shp, dt: stA.enter_context(nc.sbuf_tensor(name, shp, dt))
        wqkv_sb = sbA("wqkv_sb", [128, ND * 768], BF16)
        qT_sb = [sbA(f"qTa{h}", [128, S], BF16) for h in range(2)]
        kT_sb = [sbA(f"kTa{h}", [128, S], BF16) for h in range(2)]
        vaug = [sbA(f"vauga{h}", [128, NT * 132], BF16) for h in range(2)]
        lnxT = sbA("lnxTa", [128, ND * 512], BF16)
        xbuf = [sbA(f"xbufA{i}", [128, D], F32) for i in range(4)]
        lnxbuf = [sbA(f"lnxbufA{i}", [128, D], BF16) for i in range(2)]
        sqtmp = sbA("sqtmpA", [128, S], BF16)
        ptb = [sbA(f"ptbA{i}", [128, 1024], BF16) for i in range(3)]
        ctx_sb = sbA("ctx_sbA", [128, 512], BF16)
        vtmp = sbA("vtmpA", [128, 512], BF16)

        b0 = "B0" in phases and "B0S" not in phases
        if b0:
            wib = [stA.enter_context(nc.sbuf_tensor(f"wib{i}", [128, II // 4], mybir.dt.uint8))
                   for i in range(2)]
            wbb = [stA.enter_context(nc.sbuf_tensor(f"wbb{i}", [128, II // 4], BF16))
                   for i in range(2)]

        ph = Ph(nc)
        ldw = SM("ldwA")
        smA = dict(ldx=SM("ldxA"), vst=SM("vstA"), sln=SM("slnA"), apl=SM("aplA"),
                   ttr=SM("ttrA"), tev=SM("tevA"))
        ldx, vst, sln, apl, ttr, tev = (smA[k] for k in ("ldx", "vst", "sln", "apl", "ttr", "tev"))
        tqk = SM("tqkA"); eqk = SM("eqkA")
        tS = SM("tSA"); sP = SM("sPA"); tC = SM("tCA")
        vC = SM("vCA"); tT = SM("tTA"); vE = SM("vEA"); nrm = SM("nrmA")

        E(ph, "gpsimd", lambda e: e.dma_start(
              out=wqkv_sb[:].rearrange("p (a f) -> p a f", a=ND),
              in_=wqkvT.rearrange("(a p) f -> p a f", p=128)),
          inc=ldw, inck=16)
        W(ph, "tensor", [(ldw, ldw.n)])

        lnxT3 = lnxT[:].rearrange("p (d f) -> p d f", d=ND)
        va3s = [vaug[h][:].rearrange("p (a f) -> p a f", a=NT) for h in range(2)]

        if b0:
            ldq = SM("ldqB0"); dqc = SM("dqcB0"); spl = SM("splB0")
            wg4 = wgscr.rearrange("(ii d p) f -> d p ii f", ii=NI, p=128)
            wu4 = wuscr.rearrange("(ii d p) f -> d p ii f", ii=NI, p=128)
            wd4 = wdscr.rearrange("(ec ii p) f -> ii p ec f", ec=4, p=128)
            b0_jobs = ([("g", dt, hf) for dt in range(ND) for hf in range(4)]
                       + [("u", dt, hf) for dt in range(ND) for hf in range(4)]
                       + [("d", it, hf) for it in range(NI) for hf in range(4)])
            b0_state = {"idx": 0, "spl_after": [], "deq_after": []}

            def emit_b0_job():
                idx = b0_state["idx"]
                if idx >= len(b0_jobs):
                    return
                b0_state["idx"] += 1
                mat, t, hf = b0_jobs[idx]
                bi = idx % 2
                ncols = II // 4 if mat in ("g", "u") else D // 4
                src = {"g": gwT, "u": uwT, "d": dwT}[mat]
                lwts = []
                if idx >= 2:
                    lwts.append((dqc, b0_state["deq_after"][idx - 2]))
                vld = E(ph, "sync", lambda e, bi=bi, src=src, t=t, hf=hf, ncols=ncols: e.dma_start(
                            out=wib[bi][:, 0:ncols],
                            in_=src[t * 128:(t + 1) * 128, hf * ncols:(hf + 1) * ncols]),
                        waits=lwts, inc=ldq, inck=16)
                wts = [(ldq, vld)]
                if idx >= 2:
                    wts.append((spl, b0_state["spl_after"][idx - 2]))
                vdq = E(ph, "vector", lambda e, bi=bi, ncols=ncols: e.tensor_scalar(
                            out=wbb[bi][:, 0:ncols], in0=wib[bi][:, 0:ncols],
                            scalar1=-128.0, scalar2=None, op0=OP.add),
                        waits=wts, inc=dqc)
                b0_state["deq_after"].append(dqc.n)
                if mat in ("g", "u"):
                    dst4 = wg4 if mat == "g" else wu4
                    E(ph, "sync", lambda e, bi=bi, dst4=dst4, t=t, hf=hf: e.dma_start(
                          out=dst4[t][:, hf * 16:(hf + 1) * 16, :],
                          in_=wbb[bi][:, 0:II // 4].rearrange("p (a f) -> p a f", a=NI // 4)),
                      waits=[(dqc, vdq)], inc=spl, inck=16)
                else:
                    E(ph, "sync", lambda e, bi=bi, t=t, hf=hf: e.dma_start(
                          out=wd4[t][:, hf:hf + 1, :],
                          in_=wbb[bi][:, 0:D // 4].rearrange("p (a f) -> p a f", a=1)),
                      waits=[(dqc, vdq)], inc=spl, inck=16)
                b0_state["spl_after"].append(spl.n)

        apl_valsA = []
        ttr_valsA = []
        for m in range(B):
            for u in range(4):
                emit_ln_chunk(
                    ph, smA,
                    [xall[(m * S + u * 512 + t4 * 128):(m * S + u * 512 + t4 * 128 + 128), :]
                     for t4 in range(4)],
                    xbuf, lnxbuf,
                    [(lambda h, t4=t4: lnxT3[:, h * 8:(h + 1) * 8, t4 * 128:(t4 + 1) * 128])
                     for t4 in range(4)],
                    apl_valsA, ttr_valsA)
                if b0:
                    for _ in range(12):
                        emit_b0_job()
                for o in range(6):
                    bk = o % 2
                    wts = []
                    if o == 0:
                        wts.append((tev, tev.n))
                    wts.append((eqk, max(0, eqk.n - 1)))
                    for dt in range(ND):
                        E(ph, "tensor", lambda e, bk=bk, dt=dt, o=o: e.matmul(
                              PF(bk), wqkv_sb[:, dt * 768 + o * 128: dt * 768 + o * 128 + 128],
                              lnxT[:, dt * 512:(dt + 1) * 512],
                              start=(dt == 0), stop=(dt == ND - 1)),
                          waits=(wts if dt == 0 else ()), inc=(tqk if dt == ND - 1 else None))
                    h = 0 if o < 3 else 1
                    kind = o % 3
                    if kind == 0:
                        dst = qT_sb[h][:, u * 512:(u + 1) * 512]
                    elif kind == 1:
                        dst = kT_sb[h][:, u * 512:(u + 1) * 512]
                    else:
                        dst = vtmp[:]
                    ewts = [(tqk, tqk.n)]
                    if kind == 2:
                        ewts.append((ttr, ttr.n))
                    E(ph, "scalar", lambda e, dst=dst, bk=bk: e.copy(out=dst, in_=PF(bk)),
                      waits=ewts, inc=eqk)
                    if kind == 2:
                        for j in range(4):
                            E(ph, "tensor", lambda e, j=j: e.transpose(
                                  PB(0)[:, j * 128:(j + 1) * 128], vtmp[:, j * 128:(j + 1) * 128],
                                  ident_sb[:]),
                              waits=([(eqk, eqk.n), (tev, tev.n)] if j == 0 else ()),
                              inc=(ttr if j == 3 else None))
                        E(ph, "vector", lambda e, h=h, u=u, va3=va3s[h]: e.tensor_copy(
                              out=va3[:, u * 4:(u + 1) * 4, 0:128],
                              in_=pb[:, 0:512].rearrange("p (j f) -> p j f", j=4)),
                          waits=[(ttr, ttr.n)], inc=tev)
            for h in range(2):
                E(ph, "vector", lambda e, h=h, va3=va3s[h]: e.memset(va3[:, :, 128:129], 1.0),
                  waits=[(tev, tev.n)], inc=tev)

            vbe_h = {}
            for h in range(2):
                vq = E(ph, "vector", lambda e, h=h: e.tensor_tensor(
                           out=sqtmp[:], in0=qT_sb[h][:], in1=qT_sb[h][:], op=OP.mult),
                       waits=[(eqk, eqk.n), (tev, tev.n), (tS, tS.n)], inc=nrm)
                for j in range(4):
                    E(ph, "tensor", lambda e, j=j: e.matmul(
                          pf[0:1, j * 512:j * 512 + 512], onesc_sb[:],
                          sqtmp[:, j * 512:(j + 1) * 512], start=True, stop=True),
                      waits=([(nrm, vq), (vC, vC.n), (sP, sP.n)] if j == 0 else ()),
                      inc=(tS if j == 3 else None))
                vqe = E(ph, "vector", lambda e: e.tensor_copy(out=rowbuf[0:1, :], in_=pf[0:1, 0:2048]),
                        waits=[(tS, tS.n)], inc=nrm)
                vqm = E(ph, "vector", lambda e: e.reduce_max(colbuf[0:1, 40:41], rowbuf[0:1, :], axis=AX.X),
                        waits=[(nrm, vqe)], inc=nrm)
                vk = E(ph, "vector", lambda e, h=h: e.tensor_tensor(
                           out=sqtmp[:], in0=kT_sb[h][:], in1=kT_sb[h][:], op=OP.mult),
                      waits=[(nrm, vqm)], inc=nrm)
                for j in range(4):
                    E(ph, "tensor", lambda e, j=j: e.matmul(
                          pf[0:1, j * 512:j * 512 + 512], onesc_sb[:],
                          sqtmp[:, j * 512:(j + 1) * 512], start=True, stop=True),
                      waits=([(nrm, vk)] if j == 0 else ()), inc=(tS if j == 3 else None))
                vke = E(ph, "vector", lambda e: e.tensor_copy(out=rowbuf[0:1, :], in_=pf[0:1, 0:2048]),
                        waits=[(tS, tS.n)], inc=nrm)
                vkm = E(ph, "vector", lambda e: e.reduce_max(colbuf[0:1, 41:42], rowbuf[0:1, :], axis=AX.X),
                        waits=[(nrm, vke)], inc=nrm)
                vp = E(ph, "vector", lambda e: e.tensor_tensor(out=colbuf[0:1, 42:43],
                                                               in0=colbuf[0:1, 40:41],
                                                               in1=colbuf[0:1, 41:42], op=OP.mult),
                      waits=[(nrm, vkm)], inc=nrm)
                sl = E(ph, "scalar", lambda e: e.activation(out=colbuf[0:1, 43:44], in_=colbuf[0:1, 42:43],
                                                            func=AF.Ln),
                       waits=[(nrm, vp)], inc=sln)
                se = E(ph, "scalar", lambda e: e.activation(out=colbuf[0:1, 44:45], in_=colbuf[0:1, 43:44],
                                                            func=AF.Exp, scale=0.5),
                       waits=[(sln, sl)], inc=sln)
                vb = E(ph, "vector", lambda e: e.tensor_scalar(out=colbuf[0:1, 45:46], in0=colbuf[0:1, 44:45],
                                                               scalar1=float(-1.01 * SCALE_ATT),
                                                               scalar2=None, op0=OP.mult),
                      waits=[(sln, se)], inc=nrm)
                vbc = E(ph, "vector", lambda e: e.tensor_copy(out=bb16[0:1, 0:1], in_=colbuf[0:1, 45:46]),
                        waits=[(nrm, vb)], inc=nrm)
                E(ph, "tensor", lambda e: e.matmul(pf[:, 0:1], onesr_sb[:], bb16[0:1, 0:1],
                                                   start=True, stop=True),
                  waits=[(nrm, vbc)], inc=tS)
                vbe_h[h] = E(ph, "vector", lambda e, h=h: e.tensor_copy(out=bcol[:, h:h + 1], in_=pf[:, 0:1]),
                             waits=[(tS, tS.n)], inc=nrm)

            for h in range(2):
                g2 = 2 * m + h
                vbe = vbe_h[1]
                CTX = lambda j: pf[:, (4 + j // 2) * 512 + (j % 2) * 130:
                                   (4 + j // 2) * 512 + (j % 2) * 130 + 129]
                CTXSUM = lambda j: pf[:, (4 + j // 2) * 512 + (j % 2) * 130 + 128:
                                      (4 + j // 2) * 512 + (j % 2) * 130 + 129]
                NP = NT // 2

                def emit_ctx_pair(p, h, exp_val):
                    for half in range(2):
                        kk = 2 * p + half
                        for j in range(4):
                            E(ph, "tensor", lambda e, j=j, half=half, kk=kk, p=p, h=h: e.matmul(
                                  CTX(j), ptb[p % 3][:, half * 512 + j * 128: half * 512 + (j + 1) * 128],
                                  vaug[h][:, kk * 132:kk * 132 + 129],
                                  start=(kk == 0 and j % 2 == 0), stop=(kk == NT - 1)),
                              waits=([(sP, exp_val)] if (half == 0 and j == 0) else ()),
                              inc=(tC if (half == 1 and j == 3) else None))

                for qc in range(4):
                    q0 = qc * 512
                    exp_vals = []
                    ctx_after = []
                    tC_base = tC.n
                    for p in range(NP):
                        bk = 2 * (p % 2)
                        for half in range(2):
                            kk = 2 * p + half
                            wts = []
                            if half == 0:
                                wts.append((sP, max(0, sP.n - 1)))
                            if p == 0 and half == 0:
                                wts.append((vC, vC.n))
                                if qc == 0:
                                    wts.append((nrm, vbe))
                            E(ph, "tensor", lambda e, bk=bk, kk=kk, half=half, h=h, q0=q0: e.matmul(
                                  PF(bk + half), kT_sb[h][:, kk * 128:(kk + 1) * 128],
                                  qT_sb[h][:, q0:q0 + 512], start=True, stop=True),
                              waits=wts, inc=(tS if half == 1 else None))
                        pv = ctx_after[p - 3] if p >= 3 else tC_base
                        ev = E(ph, "scalar", lambda e, bk=bk, h=h, p=p: e.activation(
                                   out=ptb[p % 3][:], in_=pf[:, bk * 512: bk * 512 + 1024],
                                   func=AF.Exp, scale=SCALE_ATT, bias=bcol[:, h:h + 1]),
                               waits=[(tS, tS.n), (tC, pv)], inc=sP)
                        exp_vals.append(ev)
                        if p >= 2:
                            emit_ctx_pair(p - 2, h, exp_vals[p - 2])
                            ctx_after.append(tC.n)
                    emit_ctx_pair(NP - 2, h, exp_vals[NP - 2])
                    emit_ctx_pair(NP - 1, h, exp_vals[NP - 1])
                    for j in range(4):
                        rc = E(ph, "vector", lambda e, j=j: e.reciprocal(
                                  out=colbuf[:, 48 + j:49 + j], in_=CTXSUM(j)),
                              waits=([(tC, tC.n), (tT, tT.n)] if j == 0 else []), inc=vC)
                        E(ph, "vector", lambda e, j=j: e.tensor_scalar(
                              out=ctx_sb[:, j * 128:(j + 1) * 128],
                              in0=CTX(j)[:, 0:128],
                              scalar1=colbuf[:, 48 + j:49 + j], scalar2=None, op0=OP.mult),
                          waits=[(vC, rc)], inc=vC)
                    for j in range(4):
                        E(ph, "tensor", lambda e, j=j: e.transpose(
                              PB(1)[:, j * 128:(j + 1) * 128], ctx_sb[:, j * 128:(j + 1) * 128],
                              ident_sb[:]),
                          waits=([(vC, vC.n), (vE, vE.n)] if j == 0 else ()),
                          inc=(tT if j == 3 else None))
                    E(ph, "vector", lambda e, g2=g2, q0=q0: e.tensor_copy(
                          out=ctxaT[:, g2 * 2048 + q0: g2 * 2048 + q0 + 512], in_=PB(1)[:, 0:512]),
                      waits=[(tT, tT.n)], inc=vE)
        if b0:
            while b0_state["idx"] < len(b0_jobs):
                emit_b0_job()
            W(ph, "sync", [(spl, spl.n)])
        W(ph, "vector", [(vE, vE.n)])
        ph.run()
        stA.close()

        if "A2" in phases:
            stA2 = contextlib.ExitStack()
            sbA2 = lambda name, shp, dt: stA2.enter_context(nc.sbuf_tensor(name, shp, dt))
            wproj_sb = sbA2("wproj_sb", [128, ND * 2048], BF16)
            xcb = [sbA2(f"xcb{i}", [128, D], F32) for i in range(2)]
            x1b = [sbA2(f"x1b{i}", [128, D], F32) for i in range(2)]

            ph = Ph(nc)
            ldw2 = SM("ldw2"); ldxc = SM("ldxc")
            tO = SM("tOA2"); vR = SM("vRA2"); stx = SM("stxA2")
            for g in range(4):
                E(ph, "gpsimd", lambda e, g=g: e.dma_start(
                      out=wproj_sb[:].rearrange("p (a f) -> p a f", a=ND)[:, g * 4:(g + 1) * 4, :],
                      in_=wprojT.rearrange("(a p) f -> p a f", p=128)[:, g * 4:(g + 1) * 4, :]),
                  inc=ldw2, inck=16)
            W(ph, "tensor", [(ldw2, ldw2.n)])
            for tt in range(NT):
                bi = tt % 2
                vxc = E(ph, "sync", lambda e, tt=tt, bi=bi: e.dma_start(
                            out=xcb[bi][:], in_=xc[tt * 128:(tt + 1) * 128, :]),
                        waits=[(vR, max(0, vR.n - 4))], inc=ldxc, inck=16)
                for ec in range(4):
                    bk = ec % 2
                    wts = [(vR, max(0, vR.n - 1))]
                    for cd in range(ND):
                        E(ph, "tensor", lambda e, bk=bk, cd=cd, tt=tt, ec=ec: e.matmul(
                              PF(bk), ctxaT[:, cd * 2048 + tt * 128: cd * 2048 + tt * 128 + 128],
                              wproj_sb[:, cd * 2048 + ec * 512: cd * 2048 + ec * 512 + 512],
                              start=(cd == 0), stop=False),
                          waits=(wts if cd == 0 else ()))
                    E(ph, "tensor", lambda e, bk=bk, ec=ec: e.matmul(
                          PF(bk), onesr_sb[:], bprojb_sb[0:1, ec * 512:(ec + 1) * 512],
                          start=False, stop=True),
                      inc=tO)
                    awts = [(tO, tO.n)]
                    if ec == 0:
                        awts += [(ldxc, vxc), (stx, max(0, stx.n - 16))]
                    E(ph, "vector", lambda e, bk=bk, ec=ec, bi=bi: e.tensor_tensor(
                          out=x1b[bi][:, ec * 512:(ec + 1) * 512],
                          in0=xcb[bi][:, ec * 512:(ec + 1) * 512], in1=PF(bk), op=OP.add),
                      waits=awts, inc=vR)
                E(ph, "scalar", lambda e, tt=tt, bi=bi: e.dma_start(
                      out=xs[tt * 128:(tt + 1) * 128, :], in_=x1b[bi][:]),
                  waits=[(vR, vR.n)], inc=stx, inck=16)
            W(ph, "scalar", [(stx, stx.n)])
            ph.run()
            stA2.close()
        stAk.close()


    if "B0S" in phases and "B0" in phases:
        stB0 = contextlib.ExitStack()
        wib = [stB0.enter_context(nc.sbuf_tensor(f"wibs{i}", [128, II], mybir.dt.uint8)) for i in range(2)]
        wbb = [stB0.enter_context(nc.sbuf_tensor(f"wbbs{i}", [128, II], BF16)) for i in range(2)]
        ph = Ph(nc)
        ldq = SM("ldqB0s"); dqc = SM("dqcB0s"); spl = SM("splB0s")
        wg4 = wgscr.rearrange("(ii d p) f -> d p ii f", ii=NI, p=128)
        wu4 = wuscr.rearrange("(ii d p) f -> d p ii f", ii=NI, p=128)
        wd4 = wdscr.rearrange("(ec ii p) f -> ii p ec f", ec=4, p=128)
        jobs = ([("g", dt) for dt in range(ND)] + [("u", dt) for dt in range(ND)]
                + [("d", it) for it in range(NI)])
        spl_after = []
        for idx, (mat, t) in enumerate(jobs):
            bi = idx % 2
            ncols = II if mat in ("g", "u") else D
            src = {"g": gwT, "u": uwT, "d": dwT}[mat]
            vld = E(ph, "scalar", lambda e, bi=bi, src=src, t=t, ncols=ncols: e.dma_start(
                        out=wib[bi][:, 0:ncols], in_=src[t * 128:(t + 1) * 128, :]),
                    waits=[(dqc, max(0, dqc.n - 1))], inc=ldq, inck=16)
            wts = [(ldq, vld)]
            if idx >= 2:
                wts.append((spl, spl_after[idx - 2]))
            vdq = E(ph, "vector", lambda e, bi=bi, ncols=ncols: e.tensor_scalar(
                        out=wbb[bi][:, 0:ncols], in0=wib[bi][:, 0:ncols],
                        scalar1=-128.0, scalar2=None, op0=OP.add),
                    waits=wts, inc=dqc)
            if mat in ("g", "u"):
                dst4 = wg4 if mat == "g" else wu4
                E(ph, "sync", lambda e, bi=bi, dst4=dst4, t=t: e.dma_start(
                      out=dst4[t],
                      in_=wbb[bi][:].rearrange("p (a f) -> p a f", a=NI)),
                  waits=[(dqc, vdq)], inc=spl, inck=16)
            else:
                E(ph, "sync", lambda e, bi=bi, t=t: e.dma_start(
                      out=wd4[t], in_=wbb[bi][:, 0:D].rearrange("p (a f) -> p a f", a=4)),
                  waits=[(dqc, vdq)], inc=spl, inck=16)
            spl_after.append(spl.n)
        W(ph, "sync", [(spl, spl.n)])
        ph.run()
        stB0.close()

    def phase_ffn(tag, src_ap, dst_ap):
        stF = contextlib.ExitStack()
        sbF = lambda name, shp, dt: stF.enter_context(nc.sbuf_tensor(name + tag, shp, dt))
        lnxF = sbF("lnxF", [128, ND * 512], BF16)
        dsbc_sb = sbF("dsbc_sb", [128, D], F32)
        xpool = [sbF(f"xp{i}", [128, D], F32) for i in range(4)]
        lnb = [sbF(f"lnbF{i}", [128, D], BF16) for i in range(2)]
        gwb = [sbF(f"gwb{i}", [128, ND * 128], BF16) for i in range(2)]
        uwb = [sbF(f"uwb{i}", [128, ND * 128], BF16) for i in range(2)]
        wdb = [sbF(f"wdb{i}", [128, 8 * 512], BF16) for i in range(2)]
        sgb = [sbF(f"sgb{i}", [128, 512], BF16) for i in range(2)]
        utb = [sbF(f"utb{i}", [128, 512], BF16) for i in range(2)]
        hact = sbF("hact", [128, NI * 512], BF16)
        dtmp = [sbF(f"dtmp{i}", [128, 512], F32) for i in range(2)]
        stg = [sbF(f"stg{i}", [128, 512], F32) for i in range(2)]

        ph = Ph(nc)
        smF = dict(ldx=SM("ldx" + tag), vst=SM("vst" + tag), sln=SM("sln" + tag),
                   apl=SM("apl" + tag), ttr=SM("ttr" + tag), tev=SM("tev" + tag))
        ldxF, vstF, slnF, aplF, ttrF, tevF = (smF[k] for k in ("ldx", "vst", "sln", "apl", "ttr", "tev"))
        ldgu = SM("ldgu" + tag); lddn = SM("lddn" + tag)
        tgu = SM("tgu" + tag); ssil = SM("ssil" + tag); vup = SM("vup" + tag); vh = SM("vh" + tag)
        tdn = SM("tdn" + tag); vdn = SM("vdn" + tag); stx2 = SM("stx2" + tag)

        ldds = SM("ldds" + tag)
        E(ph, "sync", lambda e: e.dma_start(out=dsbc_sb[:], in_=ds_bc[:]), inc=ldds, inck=16)
        W(ph, "vector", [(ldds, ldds.n)])

        lnxF3 = lnxF[:].rearrange("p (d f) -> p d f", d=ND)
        wgr = wgscr.rearrange("(a p) f -> a p f", p=128)
        wur = wuscr.rearrange("(a p) f -> a p f", p=128)
        wdr = wdscr.rearrange("(a p) f -> a p f", p=128)

        for u in range(4):
            for t4 in range(4):
                tg = u * 4 + t4
                vld = E(ph, "sync", lambda e, t4=t4, tg=tg: e.dma_start(
                            out=xpool[t4][:], in_=src_ap[tg * 128:(tg + 1) * 128, :]),
                        waits=[(vdn, vdn.n)], inc=ldxF, inck=16)
                for j in range(4):
                    E(ph, "vector", lambda e, j=j, t4=t4: e.bn_stats(statb[:, j * 6:j * 6 + 6],
                                                                     xpool[t4][:, j * 512:(j + 1) * 512]),
                      waits=([(ldxF, vld)] if j == 0 else ()), inc=(vstF if j == 3 else None))
                vagg = E(ph, "vector", lambda e: e.bn_aggr(statb[:, 24:26], statb[:, 0:24]),
                         waits=[(vstF, vstF.n)], inc=vstF)
                E(ph, "vector", lambda e: e.tensor_scalar(out=statb[:, 26:27], in0=statb[:, 24:25],
                                                          scalar1=-1.0, scalar2=None, op0=OP.mult),
                  waits=[(vstF, vagg)])
                vve = E(ph, "vector", lambda e: e.tensor_scalar(out=statb[:, 27:28], in0=statb[:, 25:26],
                                                                scalar1=LN_EPS, scalar2=None, op0=OP.add),
                        inc=vstF)
                s1 = E(ph, "scalar", lambda e: e.activation(out=statb[:, 28:29], in_=statb[:, 27:28], func=AF.Ln),
                       waits=[(vstF, vve)], inc=slnF)
                s2 = E(ph, "scalar", lambda e: e.activation(out=statb[:, 29:30], in_=statb[:, 28:29],
                                                            func=AF.Exp, scale=-0.5),
                       waits=[(slnF, s1)], inc=slnF)
                vn = E(ph, "vector", lambda e: e.tensor_tensor(out=statb[:, 30:31], in0=statb[:, 26:27],
                                                               in1=statb[:, 29:30], op=OP.mult),
                       waits=[(slnF, s2)], inc=vstF)
                va = E(ph, "vector", lambda e, t4=t4, lb=lnb[t4 % 2]: e.tensor_scalar(
                           out=lb[:], in0=xpool[t4][:], scalar1=statb[:, 29:30], scalar2=statb[:, 30:31],
                           op0=OP.mult, op1=OP.add),
                       waits=[(vstF, vn), (ttrF, ttrF.n)], inc=aplF)
                for h in range(2):
                    wts = ([(aplF, va)] if h == 0 else []) + [(tevF, tevF.n)]
                    for j in range(8):
                        dtile = h * 8 + j
                        E(ph, "tensor", lambda e, h=h, j=j, dtile=dtile, lb=lnb[t4 % 2]: e.transpose(
                              PB(h)[:, j * 128:(j + 1) * 128], lb[:, dtile * 128:(dtile + 1) * 128],
                              ident_sb[:]),
                          waits=(wts if j == 0 else ()), inc=(ttrF if j == 7 else None))
                    E(ph, "vector", lambda e, h=h, t4=t4: e.tensor_copy(
                          out=lnxF3[:, h * 8:(h + 1) * 8, t4 * 128:(t4 + 1) * 128],
                          in_=PB(h)[:].rearrange("p (j f) -> p j f", j=8)),
                      waits=[(ttrF, ttrF.n)], inc=tevF)
            for ii in range(NI):
                bi = ii % 2
                vlg = E(ph, "sync", lambda e, bi=bi, ii=ii: e.dma_start(
                            out=gwb[bi][:].rearrange("p (a f) -> p a f", a=ND),
                            in_=wgr[ii * ND:(ii + 1) * ND].rearrange("a p f -> p a f")),
                        waits=[(tgu, max(0, tgu.n - 3))], inc=ldgu, inck=16)
                vlu = E(ph, "gpsimd", lambda e, bi=bi, ii=ii: e.dma_start(
                            out=uwb[bi][:].rearrange("p (a f) -> p a f", a=ND),
                            in_=wur[ii * ND:(ii + 1) * ND].rearrange("a p f -> p a f")),
                        waits=[(tgu, max(0, tgu.n - 2))], inc=ldgu, inck=16)
                bg, bu = (0, 1) if bi == 0 else (2, 3)
                wts0 = [(ldgu, vlg)]
                if ii == 0:
                    wts0.append((tevF, tevF.n))
                    wts0.append((vdn, vdn.n))
                wts0.append((ssil, max(0, ssil.n - 1)))
                for dt in range(ND):
                    E(ph, "tensor", lambda e, bg=bg, bi=bi, dt=dt: e.matmul(
                          PF(bg), gwb[bi][:, dt * 128:(dt + 1) * 128], lnxF[:, dt * 512:(dt + 1) * 512],
                          start=(dt == 0), stop=(dt == ND - 1)),
                      waits=(wts0 if dt == 0 else ()), inc=(tgu if dt == ND - 1 else None))
                wts1 = [(ldgu, vlu), (vup, max(0, vup.n - 1))]
                for dt in range(ND):
                    E(ph, "tensor", lambda e, bu=bu, bi=bi, dt=dt: e.matmul(
                          PF(bu), uwb[bi][:, dt * 128:(dt + 1) * 128], lnxF[:, dt * 512:(dt + 1) * 512],
                          start=(dt == 0), stop=(dt == ND - 1)),
                      waits=(wts1 if dt == 0 else ()), inc=(tgu if dt == ND - 1 else None))
                E(ph, "scalar", lambda e, bg=bg, bi=bi, ii=ii: e.activation(
                      out=sgb[bi][:], in_=PF(bg), func=AF.Silu, scale=gs_sb[:, ii:ii + 1]),
                  waits=[(tgu, tgu.n - 1), (vh, max(0, vh.n - 1))], inc=ssil)
                vu = E(ph, "vector", lambda e, bu=bu, bi=bi, ii=ii: e.tensor_scalar(
                           out=utb[bi][:], in0=PF(bu), scalar1=us_sb[:, ii:ii + 1], scalar2=None, op0=OP.mult),
                       waits=[(tgu, tgu.n), (vh, max(0, vh.n - 1))], inc=vup)
                E(ph, "vector", lambda e, bi=bi, ii=ii: e.tensor_tensor(
                      out=hact[:, ii * 512:(ii + 1) * 512], in0=sgb[bi][:], in1=utb[bi][:], op=OP.mult),
                  waits=[(vup, vu), (ssil, ssil.n)], inc=vh)
            for ec in range(4):
                for ig in range(8):
                    bi = ig % 2
                    vld2 = E(ph, "sync", lambda e, bi=bi, ec=ec, ig=ig: e.dma_start(
                                 out=wdb[bi][:].rearrange("p (a f) -> p a f", a=8),
                                 in_=wdr[ec * NI + ig * 8: ec * NI + (ig + 1) * 8].rearrange("a p f -> p a f")),
                             waits=[(tdn, max(0, tdn.n - 1))], inc=lddn, inck=16)
                    for il in range(8):
                        ii = ig * 8 + il
                        for t4 in range(4):
                            wts = []
                            if ii == 0 and t4 == 0:
                                wts = [(vh, vh.n), (ssil, ssil.n), (vup, vup.n), (lddn, vld2)]
                                if ec > 0:
                                    wts.append((vdn, vdn.n))
                            elif il == 0 and t4 == 0:
                                wts = [(lddn, vld2)]
                            E(ph, "tensor", lambda e, t4=t4, ii=ii, il=il, bi=bi: e.matmul(
                                  PF(t4), hact[:, ii * 512 + t4 * 128: ii * 512 + t4 * 128 + 128],
                                  wdb[bi][:, il * 512:(il + 1) * 512],
                                  start=(ii == 0), stop=(ii == NI - 1)),
                              waits=wts, inc=(tdn if (il == 7 and t4 == 3) else None))
                for t4 in range(4):
                    bi2 = t4 % 2
                    vd1 = E(ph, "vector", lambda e, t4=t4, ec=ec, bi2=bi2: e.tensor_tensor(
                                out=dtmp[bi2][:], in0=PF(t4), in1=dsbc_sb[:, ec * 512:(ec + 1) * 512],
                                op=OP.mult),
                            waits=([(tdn, tdn.n)] if t4 == 0 else []),
                            inc=vdn)
                    E(ph, "vector", lambda e, t4=t4, ec=ec, bi2=bi2: e.tensor_tensor(
                          out=stg[bi2][:], in0=xpool[t4][:, ec * 512:(ec + 1) * 512], in1=dtmp[bi2][:],
                          op=OP.add),
                      waits=[(vdn, vd1), (stx2, max(0, stx2.n - 16))], inc=vdn)
                    E(ph, "scalar", lambda e, t4=t4, ec=ec, u=u, bi2=bi2: e.dma_start(
                          out=dst_ap[(u * 4 + t4) * 128:(u * 4 + t4) * 128 + 128, ec * 512:(ec + 1) * 512],
                          in_=stg[bi2][:]),
                      waits=[(vdn, vdn.n)], inc=stx2, inck=16)
        W(ph, "scalar", [(stx2, stx2.n)])
        ph.run()
        stF.close()

    if "B" in phases:
        phase_ffn("B", xs, xs)

    if "C" in phases:
        stC = contextlib.ExitStack()
        sbC = lambda name, shp, dt: stC.enter_context(nc.sbuf_tensor(name, shp, dt))
        lnx2T = sbC("lnx2T", [128, ND * S], BF16)
        masks_sb = sbC("masks_sb", [128, 8 * 512], BF16)
        ropec_sb = sbC("ropec_sb", [128, NT * 64], F32)
        ropes_sb = sbC("ropes_sb", [128, NT * 64], F32)
        qwb = [sbC(f"qwbC{i}", [128, ND * 512], BF16) for i in range(2)]
        kT_all = sbC("kT_all", [128, KVH * S], BF16)
        vaug4 = sbC("vaug4", [128, KVH * NT * 132], BF16)
        qT_grp = sbC("qT_grp", [128, 4 * S], BF16)
        ktm = [sbC(f"ktmC{i}", [128, 512], F32) for i in range(3)]
        robuf = [sbC(f"robufC{i}", [128, 512], BF16) for i in range(4)]
        sqd = sbC("sqdC", [128, 256], BF16)
        rtmp = [sbC(f"rtmpC{i}", [128, 64], F32) for i in range(6)]
        ptb2 = [sbC(f"ptbC{i}", [128, 1024], BF16) for i in range(3)]
        ctx_sb2 = sbC("ctx_sbC", [128, 512], BF16)
        ctmp = [sbC(f"ctmpC{i}", [128, 512], BF16) for i in range(2)]
        xbC0 = sbC("xbC0", [128, D], F32)
        lnbC0 = sbC("lnbC0", [128, D], BF16)
        xbC = [xbC0, xbC0]
        lnbC = [lnbC0, lnbC0]

        ph = Ph(nc)
        smC = dict(ldx=SM("ldxC"), vst=SM("vstC"), sln=SM("slnC"), apl=SM("aplC"),
                   ttr=SM("ttrC"), tev=SM("tevC"))
        ldxC, vstC, slnC, aplC, ttrC, tevC = (smC[k] for k in ("ldx", "vst", "sln", "apl", "ttr", "tev"))
        ldwq = SM("ldwqC")
        tqv = SM("tqvC")
        eqv = SM("eqvC")
        vro = SM("vroC")
        sroC = SM("sroC")
        tS2 = SM("tS2C"); sP2 = SM("sP2C"); vM2 = SM("vM2C"); tC2 = SM("tC2C")
        vC2 = SM("vC2C"); tT2 = SM("tT2C"); vE2 = SM("vE2C"); stc = SM("stcC")

        lnx2T3 = lnx2T[:].rearrange("p (d f) -> p d f", d=ND)
        va43 = vaug4[:].rearrange("p (j a f) -> p j a f", j=KVH, a=NT)
        masks3 = masks_sb[:].rearrange("p (a f) -> p a f", a=8)

        ldcc = SM("ldccC")
        E(ph, "sync", lambda e: e.dma_start(out=masks_sb[:].rearrange("p (a f) -> p a f", a=8),
                                            in_=maskb.rearrange("(a p) f -> p a f", p=128)),
          inc=ldcc, inck=16)
        E(ph, "sync", lambda e: e.dma_start(out=ropec_sb[:].rearrange("p (a f) -> p a f", a=NT),
                                            in_=ropec.rearrange("(a p) f -> p a f", p=128)),
          inc=ldcc, inck=16)
        E(ph, "sync", lambda e: e.dma_start(out=ropes_sb[:].rearrange("p (a f) -> p a f", a=NT),
                                            in_=ropes.rearrange("(a p) f -> p a f", p=128)),
          inc=ldcc, inck=16)
        W(ph, "vector", [(ldcc, ldcc.n)])

        sab = E(ph, "scalar", lambda e: e.activation(out=colbuf[:, 20:36], in_=qg_sb[:], func=AF.Abs),
                inc=sroC)
        E(ph, "vector", lambda e: e.tensor_scalar(
              out=bcol[:, 2:18], in0=colbuf[:, 20:36],
              scalar1=float(-1.005 * 128.0 * SCALE_ATT), scalar2=None, op0=OP.mult),
          waits=[(sroC, sab)], inc=vro)

        for tg in range(NT):
            bi = tg % 2
            emit_ln_tile(ph, smC, xs[tg * 128:(tg + 1) * 128, :], xbC[bi], lnbC[bi],
                         lambda h, tg=tg: lnx2T3[:, h * 8:(h + 1) * 8, tg * 128:(tg + 1) * 128],
                         nbuf=1)

        E(ph, "gpsimd", lambda e: e.dma_start(
              out=qwb[0][:].rearrange("p (a f) -> p a f", a=ND),
              in_=wkT.rearrange("(a p) f -> p a f", p=128)), inc=ldwq, inck=16)
        E(ph, "gpsimd", lambda e: e.dma_start(
              out=qwb[1][:].rearrange("p (a f) -> p a f", a=ND),
              in_=wvT.rearrange("(a p) f -> p a f", p=128)), inc=ldwq, inck=16)

        def emit_rms_rope(tt, src_f32, gain_slice, dstT, dst_off, kvh_mode):
            for j in range(4):
                E(ph, "vector", lambda e, j=j: e.tensor_tensor(
                      out=sqd[:, (j % 2) * 128:(j % 2) * 128 + 128],
                      in0=src_f32[:, j * 128:(j + 1) * 128],
                      in1=src_f32[:, j * 128:(j + 1) * 128], op=OP.mult))
                if j >= 1:
                    E(ph, "vector", lambda e, j=j: e.reduce_sum(
                          colbuf[:, 60 + j - 1:61 + j - 1],
                          sqd[:, ((j - 1) % 2) * 128:((j - 1) % 2) * 128 + 128], axis=AX.X))
            E(ph, "vector", lambda e: e.reduce_sum(colbuf[:, 63:64], sqd[:, 128:256], axis=AX.X),
              inc=vro)
            vme = E(ph, "vector", lambda e: e.tensor_scalar(
                        out=colbuf[:, 52:56], in0=colbuf[:, 60:64],
                        scalar1=1.0 / 128.0, scalar2=RMS_EPS, op0=OP.mult, op1=OP.add),
                    waits=[(vro, vro.n)], inc=vro)
            l1 = E(ph, "scalar", lambda e: e.activation(out=colbuf[:, 56:60], in_=colbuf[:, 52:56], func=AF.Ln),
                   waits=[(vro, vme)], inc=sroC)
            l2 = E(ph, "scalar", lambda e: e.activation(out=colbuf[:, 52:56], in_=colbuf[:, 56:60],
                                                        func=AF.Exp, scale=-0.5),
                   waits=[(sroC, l1)], inc=sroC)
            if gain_slice is not None:
                vg = E(ph, "vector", lambda e: e.tensor_tensor(out=colbuf[:, 56:60], in0=colbuf[:, 52:56],
                                                               in1=gain_slice, op=OP.mult),
                       waits=[(sroC, l2)], inc=vro)
                rcol = colbuf[:, 56:60]
                rwait = (vro, vg)
            else:
                rcol = colbuf[:, 52:56]
                rwait = (sroC, l2)
            rob = robuf[tt % 4]
            ct = ropec_sb[:, tt * 64:(tt + 1) * 64]
            sn = ropes_sb[:, tt * 64:(tt + 1) * 64]
            for jp in (0, 2):
                for j in (jp, jp + 1):
                    x1 = src_f32[:, j * 128:j * 128 + 64]
                    x2 = src_f32[:, j * 128 + 64:(j + 1) * 128]
                    q_ = 3 * (j - jp)
                    E(ph, "vector", lambda e, x1=x1, ct=ct, q_=q_: e.tensor_tensor(
                          out=rtmp[q_][:], in0=x1, in1=ct, op=OP.mult),
                      waits=([rwait, (ttrC, ttrC.n)] if (j == jp and jp == 0) else ()))
                    E(ph, "vector", lambda e, x2=x2, sn=sn, q_=q_: e.tensor_tensor(
                          out=rtmp[q_ + 1][:], in0=x2, in1=sn, op=OP.mult))
                    E(ph, "vector", lambda e, x1=x1, sn=sn, q_=q_: e.tensor_tensor(
                          out=rtmp[q_ + 2][:], in0=x1, in1=sn, op=OP.mult))
                for j in (jp, jp + 1):
                    q_ = 3 * (j - jp)
                    x2 = src_f32[:, j * 128 + 64:(j + 1) * 128]
                    E(ph, "vector", lambda e, q_=q_: e.tensor_tensor(
                          out=rtmp[q_][:], in0=rtmp[q_][:], in1=rtmp[q_ + 1][:], op=OP.add))
                    E(ph, "vector", lambda e, x2=x2, ct=ct, q_=q_: e.tensor_tensor(
                          out=rtmp[q_ + 1][:], in0=x2, in1=ct, op=OP.mult))
                E(ph, "vector", lambda e: e.tensor_tensor(
                      out=rtmp[1][:], in0=rtmp[1][:], in1=rtmp[2][:], op=OP.subtract))
                E(ph, "vector", lambda e, rob=rob, jp=jp, sc=rcol[:, jp:jp + 1]: e.tensor_scalar(
                      out=rob[:, jp * 128:jp * 128 + 64], in0=rtmp[0][:],
                      scalar1=sc, scalar2=None, op0=OP.mult))
                E(ph, "vector", lambda e: e.tensor_tensor(
                      out=rtmp[4][:], in0=rtmp[4][:], in1=rtmp[5][:], op=OP.subtract))
                E(ph, "vector", lambda e, rob=rob, jp=jp, sc=rcol[:, jp + 1:jp + 2]: e.tensor_scalar(
                      out=rob[:, (jp + 1) * 128:(jp + 1) * 128 + 64], in0=rtmp[3][:],
                      scalar1=sc, scalar2=None, op0=OP.mult))
                E(ph, "vector", lambda e, rob=rob, jp=jp, sc=rcol[:, jp:jp + 1]: e.tensor_scalar(
                      out=rob[:, jp * 128 + 64:jp * 128 + 128], in0=rtmp[1][:],
                      scalar1=sc, scalar2=None, op0=OP.mult))
                E(ph, "vector", lambda e, rob=rob, jp=jp, sc=rcol[:, jp + 1:jp + 2]: e.tensor_scalar(
                      out=rob[:, (jp + 1) * 128 + 64:(jp + 2) * 128], in0=rtmp[4][:],
                      scalar1=sc, scalar2=None, op0=OP.mult),
                  inc=(vro if jp == 2 else None))
            return vro.n

        def emit_rope_T(tt, dstT, rope_val):
            rob = robuf[tt % 4]
            for j in range(4):
                E(ph, "tensor", lambda e, j=j, rob=rob: e.transpose(
                      PB(1)[:, j * 128:(j + 1) * 128], rob[:, j * 128:(j + 1) * 128], ident_sb[:]),
                  waits=([(vro, rope_val), (tevC, tevC.n)] if j == 0 else ()),
                  inc=(ttrC if j == 3 else None))
            E(ph, "vector", lambda e, dstT=dstT, tt=tt: e.tensor_copy(
                  out=dstT[:].rearrange("p (j f) -> p j f", j=4)[:, :, tt * 128:(tt + 1) * 128],
                  in_=pb[:, 1024:1536].rearrange("p (j f) -> p j f", j=4)),
              waits=[(ttrC, ttrC.n)], inc=tevC)

        rope_vals = {}
        for tt in range(NT):
            bk = tt % 2
            wts = [(eqv, max(0, eqv.n - 1))]
            if tt == 0:
                wts += [(tevC, tevC.n), (ldwq, 16)]
            for dt in range(ND):
                E(ph, "tensor", lambda e, bk=bk, dt=dt, tt=tt: e.matmul(
                      PF(bk), lnx2T3[:, dt, tt * 128:(tt + 1) * 128],
                      qwb[0][:, dt * 512:(dt + 1) * 512],
                      start=(dt == 0), stop=(dt == ND - 1)),
                  waits=(wts if dt == 0 else ()), inc=(tqv if dt == ND - 1 else None))
            if tt >= 1:
                emit_rope_T(tt - 1, kT_all, rope_vals[tt - 1])
            ek = E(ph, "scalar", lambda e, bk=bk, tt=tt: e.copy(out=ktm[tt % 3][:], in_=PF(bk)),
                   waits=[(tqv, tqv.n), (vro, rope_vals.get(tt - 3, 0))], inc=eqv)
            W(ph, "vector", [(eqv, ek)])
            rope_vals[tt] = emit_rms_rope(tt, ktm[tt % 3], None, kT_all, 0, True)
            wts = [(eqv, max(0, eqv.n - 1)), (ldwq, 32)]
            for dt in range(ND):
                E(ph, "tensor", lambda e, bk=bk, dt=dt, tt=tt: e.matmul(
                      PF(2 + bk), lnx2T3[:, dt, tt * 128:(tt + 1) * 128],
                      qwb[1][:, dt * 512:(dt + 1) * 512],
                      start=(dt == 0), stop=(dt == ND - 1)),
                  waits=(wts if dt == 0 else ()), inc=(tqv if dt == ND - 1 else None))
            E(ph, "scalar", lambda e, bk=bk, tt=tt: e.copy(
                  out=va43[:, :, tt, 0:128],
                  in_=PF(2 + bk).rearrange("p (j f) -> p j f", j=4)),
              waits=[(tqv, tqv.n)], inc=eqv)
        emit_rope_T(NT - 1, kT_all, rope_vals[NT - 1])
        E(ph, "vector", lambda e: e.memset(va43[:, :, :, 128:129], 1.0),
          waits=[(eqv, eqv.n)], inc=tevC)

        for ec in range(4):
            bi = ec % 2
            vlq = E(ph, "gpsimd", lambda e, ec=ec, bi=bi: e.dma_start(
                        out=qwb[bi][:].rearrange("p (a f) -> p a f", a=ND),
                        in_=wqT[:, ec * 512:(ec + 1) * 512].rearrange("(a p) f -> p a f", p=128)),
                    waits=[(tqv, tqv.n)], inc=ldwq, inck=16)
            rope_vals_q = {}
            ec_base = vro.n
            for tt in range(NT):
                bk = tt % 2
                wts = [(eqv, max(0, eqv.n - 1)), (ldwq, vlq)]
                for dt in range(ND):
                    E(ph, "tensor", lambda e, bk=bk, dt=dt, tt=tt, bi=bi: e.matmul(
                          PF(bk), lnx2T3[:, dt, tt * 128:(tt + 1) * 128],
                          qwb[bi][:, dt * 512:(dt + 1) * 512],
                          start=(dt == 0), stop=(dt == ND - 1)),
                      waits=(wts if dt == 0 else ()), inc=(tqv if dt == ND - 1 else None))
                if tt >= 2:
                    emit_rope_T(tt - 2, qT_grp, rope_vals_q[tt - 2])
                eq = E(ph, "scalar", lambda e, bk=bk, tt=tt: e.copy(out=ktm[tt % 3][:], in_=PF(bk)),
                       waits=[(tqv, tqv.n), (vro, rope_vals_q.get(tt - 3, ec_base))], inc=eqv)
                W(ph, "vector", [(eqv, eq)])
                rope_vals_q[tt] = emit_rms_rope(tt, ktm[tt % 3], qg_sb[:, ec * 4:(ec + 1) * 4],
                                                qT_grp, 0, False)
            emit_rope_T(NT - 2, qT_grp, rope_vals_q[NT - 2])
            emit_rope_T(NT - 1, qT_grp, rope_vals_q[NT - 1])
            CTX = lambda j: pf[:, (4 + j // 2) * 512 + (j % 2) * 130:
                               (4 + j // 2) * 512 + (j % 2) * 130 + 129]
            CTXSUM = lambda j: pf[:, (4 + j // 2) * 512 + (j % 2) * 130 + 128:
                                  (4 + j // 2) * 512 + (j % 2) * 130 + 129]
            for hq in range(4):
                head = ec * 4 + hq
                for qc in range(4):
                    q0 = qc * 512
                    kklo = max(0, (qc - 1) * 4)
                    kkhi = qc * 4 + 3
                    pairs = [(kklo + 2 * i, kklo + 2 * i + 1) for i in range((kkhi + 1 - kklo) // 2)]
                    npair = len(pairs)

                    def emit_ctx_pair_b(pi, mask_val, ec=ec, kklo=kklo, kkhi=kkhi, pairs=pairs):
                        for half in range(2):
                            kk = pairs[pi][half]
                            for j in range(4):
                                E(ph, "tensor", lambda e, j=j, half=half, kk=kk, pi=pi, ec=ec: e.matmul(
                                      CTX(j), ptb2[pi % 3][:, half * 512 + j * 128: half * 512 + (j + 1) * 128],
                                      vaug4[:, ec * (NT * 132) + kk * 132: ec * (NT * 132) + kk * 132 + 129],
                                      start=(kk == kklo and j % 2 == 0), stop=(kk == kkhi)),
                                  waits=([(vM2, mask_val)] if (half == 0 and j == 0) else ()),
                                  inc=(tC2 if (half == 1 and j == 3) else None))

                    mask_vals = []
                    ctxb_after = []
                    chunk_base = tC2.n
                    for pi in range(npair):
                        bk2 = 2 * (pi % 2)
                        for half in range(2):
                            kk = pairs[pi][half]
                            wts = []
                            if half == 0:
                                wts.append((sP2, max(0, sP2.n - 1)))
                            if pi == 0 and half == 0:
                                wts.append((vC2, vC2.n))
                                wts.append((eqv, eqv.n))
                                if ec == 0 and hq == 0 and qc == 0:
                                    wts.append((tevC, tevC.n))
                            E(ph, "tensor", lambda e, bk2=bk2, kk=kk, half=half, ec=ec, hq=hq, q0=q0: e.matmul(
                                  PF(bk2 + half), kT_all[:, ec * 2048 + kk * 128: ec * 2048 + (kk + 1) * 128],
                                  qT_grp[:, hq * 2048 + q0: hq * 2048 + q0 + 512],
                                  start=True, stop=True),
                              waits=wts, inc=(tS2 if half == 1 else None))
                        pvb = ctxb_after[pi - 3] if pi >= 3 else chunk_base
                        E(ph, "scalar", lambda e, bk2=bk2, head=head, pi=pi: e.activation(
                              out=ptb2[pi % 3][:], in_=pf[:, bk2 * 512: bk2 * 512 + 1024],
                              func=AF.Exp, scale=SCALE_ATT, bias=bcol[:, 2 + head:3 + head]),
                          waits=[(tS2, tS2.n), (tC2, pvb)], inc=sP2)
                        for half in range(2):
                            kk = pairs[pi][half]
                            midx = (512 - (q0 - kk * 128)) // 128
                            mv = E(ph, "vector", lambda e, pi=pi, half=half, midx=midx: e.tensor_tensor(
                                       out=ptb2[pi % 3][:, half * 512:(half + 1) * 512],
                                       in0=ptb2[pi % 3][:, half * 512:(half + 1) * 512],
                                       in1=masks3[:, midx, :], op=OP.mult),
                                   waits=([(sP2, sP2.n)] if half == 0 else ()), inc=vM2)
                        mask_vals.append(mv)
                        if pi >= 2:
                            emit_ctx_pair_b(pi - 2, mask_vals[pi - 2])
                            ctxb_after.append(tC2.n)
                    if npair >= 2:
                        emit_ctx_pair_b(npair - 2, mask_vals[npair - 2])
                    emit_ctx_pair_b(npair - 1, mask_vals[npair - 1])
                    for j in range(4):
                        rc = E(ph, "vector", lambda e, j=j: e.reciprocal(
                                  out=colbuf[:, 48 + j:49 + j], in_=CTXSUM(j)),
                              waits=([(tC2, tC2.n), (tT2, tT2.n)] if j == 0 else []), inc=vC2)
                        E(ph, "vector", lambda e, j=j: e.tensor_scalar(
                              out=ctx_sb2[:, j * 128:(j + 1) * 128],
                              in0=CTX(j)[:, 0:128],
                              scalar1=colbuf[:, 48 + j:49 + j], scalar2=None, op0=OP.mult),
                          waits=[(vC2, rc)], inc=vC2)
                    for j in range(4):
                        E(ph, "tensor", lambda e, j=j: e.transpose(
                              PB(0)[:, j * 128:(j + 1) * 128], ctx_sb2[:, j * 128:(j + 1) * 128],
                              ident_sb[:]),
                          waits=([(vC2, vC2.n), (vE2, vE2.n)] if j == 0 else ()),
                          inc=(tT2 if j == 3 else None))
                    ce = E(ph, "vector", lambda e, qc=qc: e.tensor_copy(
                               out=ctmp[qc % 2][:], in_=pb[:, 0:512]),
                           waits=[(tT2, tT2.n), (stc, max(0, stc.n - 16))], inc=vE2)
                    E(ph, "scalar", lambda e, head=head, q0=q0, qc=qc: e.dma_start(
                          out=ctxb_scr[head * 128:(head + 1) * 128, q0:q0 + 512], in_=ctmp[qc % 2][:]),
                      waits=[(vE2, ce)], inc=stc, inck=16)
        W(ph, "scalar", [(stc, stc.n)])
        ph.run()
        stC.close()

        if "C2" in phases:
            stC2 = contextlib.ExitStack()
            sbC2 = lambda name, shp, dt: stC2.enter_context(nc.sbuf_tensor(name, shp, dt))
            cxT_sb = sbC2("cxT_sb", [128, NH * 2048], BF16)
            wo_sb = sbC2("wo_sb", [128, ND * 2048], BF16)
            xcb2 = [sbC2(f"xcb2{i}", [128, D], F32) for i in range(2)]
            x3b = [sbC2(f"x3b{i}", [128, D], F32) for i in range(2)]

            ph = Ph(nc)
            ldw3 = SM("ldw3"); ldxc2 = SM("ldxc2")
            tO2 = SM("tOC2"); vR2 = SM("vRC2"); stx3 = SM("stxC2")
            for g in range(4):
                E(ph, "gpsimd", lambda e, g=g: e.dma_start(
                      out=wo_sb[:].rearrange("p (a f) -> p a f", a=ND)[:, g * 4:(g + 1) * 4, :],
                      in_=woT.rearrange("(a p) f -> p a f", p=128)[:, g * 4:(g + 1) * 4, :]),
                  inc=ldw3, inck=16)
            for g in range(4):
                E(ph, "sync", lambda e, g=g: e.dma_start(
                      out=cxT_sb[:].rearrange("p (a f) -> p a f", a=NH)[:, g * 4:(g + 1) * 4, :],
                      in_=ctxb_scr.rearrange("(a p) f -> p a f", p=128)[:, g * 4:(g + 1) * 4, :]),
                  inc=ldw3, inck=16)
            W(ph, "tensor", [(ldw3, ldw3.n)])
            for tt in range(NT):
                bi = tt % 2
                vxc = E(ph, "sync", lambda e, tt=tt, bi=bi: e.dma_start(
                            out=xcb2[bi][:], in_=xs[tt * 128:(tt + 1) * 128, :]),
                        waits=[(vR2, max(0, vR2.n - 4))], inc=ldxc2, inck=16)
                for ec in range(4):
                    bk = ec % 2
                    wts = [(vR2, max(0, vR2.n - 1))]
                    for cd in range(ND):
                        E(ph, "tensor", lambda e, bk=bk, cd=cd, tt=tt, ec=ec: e.matmul(
                              PF(bk), cxT_sb[:, cd * 2048 + tt * 128: cd * 2048 + tt * 128 + 128],
                              wo_sb[:, cd * 2048 + ec * 512: cd * 2048 + ec * 512 + 512],
                              start=(cd == 0), stop=(cd == ND - 1)),
                          waits=(wts if cd == 0 else ()), inc=(tO2 if cd == ND - 1 else None))
                    awts = [(tO2, tO2.n)]
                    if ec == 0:
                        awts += [(ldxc2, vxc), (stx3, max(0, stx3.n - 16))]
                    E(ph, "vector", lambda e, bk=bk, ec=ec, bi=bi: e.tensor_tensor(
                          out=x3b[bi][:, ec * 512:(ec + 1) * 512],
                          in0=xcb2[bi][:, ec * 512:(ec + 1) * 512], in1=PF(bk), op=OP.add),
                      waits=awts, inc=vR2)
                E(ph, "scalar", lambda e, tt=tt, bi=bi: e.dma_start(
                      out=xs[tt * 128:(tt + 1) * 128, :], in_=x3b[bi][:]),
                  waits=[(vR2, vR2.n)], inc=stx3, inck=16)
            W(ph, "scalar", [(stx3, stx3.n)])
            ph.run()
            stC2.close()

    if "D" in phases:
        phase_ffn("D", xs, y)

    if "DBG" in phases:
        stDb = contextlib.ExitStack()
        db = stDb.enter_context(nc.sbuf_tensor("dbgb", [128, 512], BF16))
        dbf = stDb.enter_context(nc.sbuf_tensor("dbgf", [128, 512], F32))
        ph = Ph(nc)
        ldd = SM("ldDBG"); vcd = SM("vcDBG"); std = SM("stDBG")
        for blk in range(4):
            for r in range(16):
                vl = E(ph, "sync", lambda e, blk=blk, r=r: e.dma_start(
                           out=db[:], in_=wdscr[blk * 8192 + r * 128: blk * 8192 + (r + 1) * 128, :]),
                       waits=[(vcd, vcd.n)], inc=ldd, inck=16)
                vc = E(ph, "vector", lambda e: e.tensor_copy(out=dbf[:], in_=db[:]),
                       waits=[(ldd, vl), (std, std.n)], inc=vcd)
                E(ph, "scalar", lambda e, blk=blk, r=r: e.dma_start(
                      out=y[r * 128:(r + 1) * 128, blk * 512:(blk + 1) * 512], in_=dbf[:]),
                  waits=[(vcd, vc)], inc=std, inck=16)
        W(ph, "scalar", [(std, std.n)])
        ph.run()
        stDb.close()

    if "E" in phases:
        stE = contextlib.ExitStack()
        ob = [stE.enter_context(nc.sbuf_tensor(f"ob{i}", [128, D], F32)) for i in range(2)]
        ph = Ph(nc)
        lde = SM("ldE"); ste = SM("stE")
        for tt in range(NT):
            bi = tt % 2
            vl = E(ph, "sync", lambda e, tt=tt, bi=bi: e.dma_start(
                       out=ob[bi][:], in_=xs[tt * 128:(tt + 1) * 128, :]),
                   waits=[(ste, max(0, ste.n - 16))], inc=lde, inck=16)
            E(ph, "sync", lambda e, tt=tt, bi=bi: e.dma_start(
                  out=y[tt * 128:(tt + 1) * 128, :], in_=ob[bi][:]),
              waits=[(lde, vl)], inc=ste, inck=16)
        W(ph, "sync", [(ste, ste.n)])
        ph.run()
        stE.close()

    st.close()
    return nc



def _host_prep(inputs):
    f32 = np.float32
    _ = inputs["L0_attn_a_key"], inputs["L0_attn_a_value"]
    x = np.asarray(inputs["L0_attn_a_query"], f32)
    Wqkv = np.asarray(inputs["L0_attn_a_Wqkv"], f32)
    Wproj = np.asarray(inputs["L0_attn_a_Wproj"], f32)
    bproj = np.asarray(inputs["L0_attn_a_bproj"], f32)
    Wq = np.asarray(inputs["L0_attn_b_Wq"], f32)
    Wk = np.asarray(inputs["L0_attn_b_Wk"], f32)
    Wv = np.asarray(inputs["L0_attn_b_Wv"], f32)
    Wo = np.asarray(inputs["L0_attn_b_Wo"], f32)
    qg = np.asarray(inputs["L0_attn_b_q_gain"], f32)
    gw = np.asarray(inputs["L0_ffn_gate_w"], np.int32)
    gs = np.asarray(inputs["L0_ffn_gate_scale"], f32)
    uw = np.asarray(inputs["L0_ffn_up_w"], np.int32)
    us = np.asarray(inputs["L0_ffn_up_scale"], f32)
    dw = np.asarray(inputs["L0_ffn_down_w"], np.int32)
    ds = np.asarray(inputs["L0_ffn_down_scale"], f32)

    shared = {}
    shared["xall"] = np.ascontiguousarray(x.reshape(B * S, D))
    shared["wprojT"] = np.ascontiguousarray(Wproj.T)
    shared["bprojT"] = np.ascontiguousarray(bproj[None, :])
    shared["wqT"] = np.ascontiguousarray(Wq.T)
    shared["wkT"] = np.ascontiguousarray(Wk.T)
    shared["wvT"] = np.ascontiguousarray(Wv.T)
    shared["woT"] = np.ascontiguousarray(Wo.T)
    shared["gwT"] = np.ascontiguousarray(gw.T.astype(np.uint8))
    shared["uwT"] = np.ascontiguousarray(uw.T.astype(np.uint8))
    shared["dwT"] = np.ascontiguousarray(dw.T.astype(np.uint8))
    shared["gs_col"] = np.ascontiguousarray(gs.reshape(NI, 128).T)
    shared["us_col"] = np.ascontiguousarray(us.reshape(NI, 128).T)
    shared["ds_bc"] = np.ascontiguousarray(np.broadcast_to(ds[None, :], (128, D)))
    shared["qg_bc"] = np.ascontiguousarray(np.broadcast_to(qg[None, :], (128, NH)))
    shared["identb"] = np.eye(128, dtype=ml_dtypes.bfloat16)
    inv_freq = 10000.0 ** (-np.arange(0, 64, dtype=np.float64) / 64.0)
    freqs = np.arange(S, dtype=np.float64)[:, None] * inv_freq[None, :]
    shared["ropec"] = np.cos(freqs).astype(f32)
    shared["ropes"] = np.sin(freqs).astype(f32)
    qi = np.arange(512)[None, :]
    ki = np.arange(128)[:, None]
    masks = []
    for r in (512, 384, 256, 128, 0, -128, -256, -384):
        dm = qi - ki + r
        masks.append(((dm >= 0) & (dm <= WIN - 1)).astype(ml_dtypes.bfloat16))
    shared["maskb"] = np.ascontiguousarray(np.concatenate(masks, 0))
    shared["ones_row"] = np.ones((1, 128), dtype=ml_dtypes.bfloat16)
    shared["ones_col"] = np.ones((128, 1), dtype=ml_dtypes.bfloat16)

    in_maps = []
    for c in range(B):
        mm = dict(shared)
        mm["xc"] = np.ascontiguousarray(x[c])
        rows = np.concatenate([Wqkv[c * 384:(c + 1) * 384], Wqkv[(c + 8) * 384:(c + 9) * 384]], 0)
        mm["wqkvT"] = np.ascontiguousarray(rows.T)
        in_maps.append(mm)
    return in_maps


def kernel(**inputs):
    in_maps = _host_prep(inputs)
    nc = _build(tuple(PHASES))
    res = run_bass_kernel_spmd(nc, in_maps, core_ids=list(range(B)))
    return np.stack([r["y"] for r in res.results], 0)
